# version 03
# baseline (speedup 1.0000x reference)
import numpy as np
import ml_dtypes

import concourse.bass as bass
import concourse.mybir as mybir
from concourse.bass_utils import run_bass_kernel_spmd

F32 = mybir.dt.float32
BF16 = mybir.dt.bfloat16
AF = mybir.ActivationFunctionType
ALU = mybir.AluOpType
AX = mybir.AxisListType
ds = bass.ds
bf16 = ml_dtypes.bfloat16

NCORES = 8
D = 1024
DC = 8
H = 16
DI = 4096
FC = 32
T = 2048
NSL = 4
EPS = 1e-5
RND = 12582912.0

_DEBUG = {"upto": None, "spill_out": False, "dbg_s": 0}


def _layout_inputs(inp):
    x0 = np.ascontiguousarray(np.asarray(inp["L0_attn_enc_input"], dtype=np.float32)[0])

    def packqkv(w):
        return np.ascontiguousarray(np.asarray(w).transpose(1, 0, 2).reshape(D, H * 64))

    def cols(v, n):
        return np.ascontiguousarray(np.asarray(v).reshape(n, 128).T)

    shared = {
        "wq_la": packqkv(inp["L0_attn_la_wqs"]), "wk_la": packqkv(inp["L0_attn_la_wks"]),
        "wv_la": packqkv(inp["L0_attn_la_wvs"]), "pw_la": np.ascontiguousarray(np.asarray(inp["L0_attn_la_pw"])),
        "wq_sa": packqkv(inp["L0_attn_sa_wqs"]), "wk_sa": packqkv(inp["L0_attn_sa_wks"]),
        "wv_sa": packqkv(inp["L0_attn_sa_wvs"]), "pw_sa": np.ascontiguousarray(np.asarray(inp["L0_attn_sa_pw"])),
        "w1_lf": np.ascontiguousarray(np.asarray(inp["L0_attn_lf_w1"])),
        "w2_lf": np.ascontiguousarray(np.asarray(inp["L0_attn_lf_w2"])),
        "w1_pf": np.ascontiguousarray(np.asarray(inp["L0_attn_pf_w1"])),
        "w2_pf": np.ascontiguousarray(np.asarray(inp["L0_attn_pf_w2"])),
        "w1t_q": np.ascontiguousarray(np.asarray(inp["L0_ffn_W1"]).T),
        "w2t_q": np.ascontiguousarray(np.asarray(inp["L0_ffn_W2"]).T),
        "pb_la": cols(inp["L0_attn_la_pb"], DC), "g_la": cols(inp["L0_attn_la_g"], DC),
        "b_la": cols(inp["L0_attn_la_b"], DC), "b1_lf": cols(inp["L0_attn_lf_b1"], FC),
        "b2_lf": cols(inp["L0_attn_lf_b2"], DC), "g_lf": cols(inp["L0_attn_lf_g"], DC),
        "b_lf": cols(inp["L0_attn_lf_b"], DC), "pb_sa": cols(inp["L0_attn_sa_pb"], DC),
        "g_sa": cols(inp["L0_attn_sa_g"], DC), "b_sa": cols(inp["L0_attn_sa_b"], DC),
        "b1_pf": cols(inp["L0_attn_pf_b1"], FC), "b2_pf": cols(inp["L0_attn_pf_b2"], DC),
        "g_pf": cols(inp["L0_attn_pf_g"], DC), "b_pf": cols(inp["L0_attn_pf_b"], DC),
        "b1_q": cols(inp["L0_ffn_b1"], FC), "b2_q": cols(inp["L0_ffn_b2"], DC),
    }
    x0r = x0.reshape(512, 32, D)
    in_maps = []
    for c in range(NCORES):
        m = dict(shared)
        m["x_win_t"] = np.ascontiguousarray(x0[2048 * c:2048 * (c + 1)].T)
        m["x_str_t"] = np.ascontiguousarray(
            x0r[:, 4 * c:4 * c + 4, :].transpose(2, 1, 0).reshape(D, T))
        in_maps.append(m)
    return in_maps


def _assemble(results):
    out = np.empty((16384, D), dtype=np.float32)
    outr = out.reshape(512, 32, D)
    for c in range(NCORES):
        y = np.asarray(results[c]["out"])
        yt = y.T.reshape(4, 512, D)
        for il in range(4):
            outr[:, 4 * c + il, :] = yt[il]
    return out.reshape(1, 16384, D)


class P:

    ENGS = ("sync", "tensor", "vector", "scalar", "gpsimd")
    NCH = 24

    def __init__(self, nc, ctx):
        self.nc = nc
        self.q = {e: [] for e in self.ENGS}
        self.sems = {}
        self.cnt = {}
        self.owner = {}
        self.lastw = {e: {} for e in self.ENGS}
        self.dcnt = {}
        for e in self.ENGS:
            name = "m" + e
            self.sems[name] = ctx.enter_context(nc.semaphore(name))
            self.cnt[name] = 0
            self.owner[name] = e
        for e in ("sync", "scalar", "gpsimd"):
            self.dcnt[e] = 0
            for i in range(self.NCH):
                name = f"d{e}{i}"
                self.sems[name] = ctx.enter_context(nc.semaphore(name))
                self.cnt[name] = 0
                self.owner[name] = e
        self.sems["cc"] = ctx.enter_context(nc.semaphore("cc"))
        self.cnt["cc"] = 0
        self.owner["cc"] = "gpsimd"
        self.last_ev = {e: None for e in self.ENGS}
        self.stage_tag = "init"

    def wait(self, eng, ev):
        if ev is None:
            return
        s, v = ev
        if self.lastw[eng].get(s, 0) >= v:
            return
        self.lastw[eng][s] = v
        self.q[eng].append(("w", s, v, self.stage_tag))

    def waits(self, eng, evs):
        for ev in evs:
            self.wait(eng, ev)

    def do(self, eng, fn, mark=False, dma=False, cc=False):
        ev = None
        sem = None
        inc = 1
        if cc:
            sem, inc = "cc", 1
            self.cnt[sem] += inc
            ev = (sem, self.cnt[sem])
        elif dma:
            i = self.dcnt[eng]
            self.dcnt[eng] = i + 1
            sem = f"d{eng}{i % self.NCH}"
            inc = 16
            if i >= self.NCH:
                self.wait(eng, (sem, self.cnt[sem]))
            self.cnt[sem] += 16
            ev = (sem, self.cnt[sem])
        elif mark:
            sem = "m" + eng
            self.cnt[sem] += 1
            ev = (sem, self.cnt[sem])
        self.q[eng].append(("i", fn, sem, inc, self.stage_tag))
        if ev is not None:
            self.last_ev[eng] = ev
        return ev

    def mark_now(self, eng):
        return self.do(eng, lambda e: e.nop(), mark=True)

    def replay(self, block):
        def mk(eng_name):
            def body(eng):
                cur = [None, None]
                def set_scope(tag):
                    if tag == cur[0]:
                        return
                    if cur[0] is not None:
                        self.nc.leave_named_scope(f"{cur[0]}_{eng_name}", cur[1], False)
                    sid, _ = self.nc.enter_named_scope(f"{tag}_{eng_name}", False)
                    cur[0], cur[1] = tag, sid
                for item in self.q[eng_name]:
                    set_scope(item[-1])
                    if item[0] == "w":
                        eng.wait_ge(self.sems[item[1]], item[2])
                    else:
                        _, fn, sem, inc, _tag = item
                        ins = fn(eng)
                        if sem is not None:
                            ins.then_inc(self.sems[sem], inc)
                if cur[0] is not None:
                    self.nc.leave_named_scope(f"{cur[0]}_{eng_name}", cur[1], False)
            return body
        block.sync(mk("sync"))
        block.tensor(mk("tensor"))
        block.vector(mk("vector"))
        block.scalar(mk("scalar"))
        block.gpsimd(mk("gpsimd"))


def build_nc():
    from contextlib import ExitStack
    nc = bass.Bass("TRN2", target_bir_lowering=False,
                   detect_race_conditions=not _DEBUG.get("no_race", False))
    upto = _DEBUG.get("upto")
    spill_kind = "ExternalOutput" if _DEBUG.get("spill_out") else "Internal"

    din = {}
    for name, shape in [
        ("x_win_t", [D, T]), ("x_str_t", [D, T]),
        ("wq_la", [D, D]), ("wk_la", [D, D]), ("wv_la", [D, D]), ("pw_la", [D, D]),
        ("wq_sa", [D, D]), ("wk_sa", [D, D]), ("wv_sa", [D, D]), ("pw_sa", [D, D]),
        ("w1_lf", [D, DI]), ("w2_lf", [DI, D]), ("w1_pf", [D, DI]), ("w2_pf", [DI, D]),
        ("w1t_q", [D, DI]), ("w2t_q", [DI, D]),
        ("pb_la", [128, DC]), ("g_la", [128, DC]), ("b_la", [128, DC]),
        ("b1_lf", [128, FC]), ("b2_lf", [128, DC]), ("g_lf", [128, DC]), ("b_lf", [128, DC]),
        ("pb_sa", [128, DC]), ("g_sa", [128, DC]), ("b_sa", [128, DC]),
        ("b1_pf", [128, FC]), ("b2_pf", [128, DC]), ("g_pf", [128, DC]), ("b_pf", [128, DC]),
        ("b1_q", [128, FC]), ("b2_q", [128, DC]),
    ]:
        din[name] = nc.dram_tensor(name, shape, F32, kind="ExternalInput").ap()
    out_d = nc.dram_tensor("out", [D, T], F32, kind="ExternalOutput").ap()

    dsc = {
        "x_win_bf": nc.dram_tensor("x_win_bf", [D, T], BF16, kind="Internal").ap(),
        "xm1": nc.dram_tensor("xm1", [D, T], BF16, kind=spill_kind).ap(),
        "xm2": nc.dram_tensor("xm2", [D, T], BF16, kind=spill_kind).ap(),
        "u2": nc.dram_tensor("u2", [D, T], F32, kind=spill_kind).ap(),
        "cc_in": nc.dram_tensor("cc_in", [T, D], BF16,
                                kind=spill_kind if upto in ("s1", "s2") else "Internal").ap(),
        "cc_out": nc.dram_tensor("cc_out", [NCORES * T, D], BF16, kind="Internal",
                                 addr_space="Shared").ap(),
        "e2tm": nc.dram_tensor("e2tm", [T, D], BF16, kind=spill_kind).ap(),
        "wq1": nc.dram_tensor("wq1", [D, DI], BF16, kind=spill_kind).ap(),
        "wq2": nc.dram_tensor("wq2", [DI, D], BF16, kind="Internal").ap(),
        "bc_ln": nc.dram_tensor("bc_ln", [1, 1024], F32, kind="Internal").ap(),
        "bc_nc": nc.dram_tensor("bc_nc", [1, 8], F32, kind="Internal").ap(),
        "bc_iv": nc.dram_tensor("bc_iv", [1, 4], F32, kind="Internal").ap(),
        "bc_q": nc.dram_tensor("bc_q", [2, 128], F32, kind="Internal").ap(),
    }
    if _DEBUG.get("spill_out"):
        for nm, shp, dt in [("dbg_e", [D, T], BF16), ("dbg_q", [128, 4096], BF16),
                            ("dbg_k", [128, 4096], BF16), ("dbg_o", [128, 4096], BF16),
                            ("dbg_negc", [128, 8], F32), ("dbg_y", [128, 4096], F32),
                            ("dbg_a", [128, 512], BF16), ("dbg_x", [128, 4096], BF16),
                            ("dbg_w", [128, 8192], BF16), ("dbg_v", [128, 4160], BF16),
                            ("dbg_rows", [1, 2048], F32), ("dbg_rb", [128, 1024], F32),
                            ("dbg_xm", [128, 4096], BF16), ("dbg_mrb", [128, 1024], F32),
                            ("dbg_mrows", [1, 2048], F32), ("dbg_f_x", [128, 4096], BF16),
                            ("dbg_f_h", [128, 8192], BF16), ("dbg_f_u", [128, 4096], F32)]:
            dsc[nm] = nc.dram_tensor(nm, shp, dt, kind="ExternalOutput").ap()
    for nm, shp in [("wq_sa", [D, D]), ("wk_sa", [D, D]), ("wv_sa", [D, D]), ("pw_sa", [D, D]),
                    ("w1_lf", [D, DI]), ("w2_lf", [DI, D]), ("w1_pf", [D, DI]), ("w2_pf", [DI, D])]:
        dsc[nm + "_bf"] = nc.dram_tensor(nm + "_bf", shp, BF16, kind="Internal").ap()

    ident = nc.inline_tensor(np.eye(128, dtype=bf16), "ident").ap()
    maskblk = np.kron(np.eye(4, dtype=np.float32), np.ones((32, 32), np.float32))
    mask512 = nc.inline_tensor(np.tile(maskblk, (1, 4)).astype(bf16), "mask512").ap()
    ones_bf_c = nc.inline_tensor(np.ones((128, 1), dtype=bf16), "ones_bf").ap()
    ones128_c = nc.inline_tensor(np.ones((128, 128), dtype=bf16), "ones128").ap()
    ones_f_c = nc.inline_tensor(np.ones((128, 1), dtype=np.float32), "ones_f").ap()
    ones_f_r = nc.inline_tensor(np.ones((1, 128), dtype=np.float32), "ones_fr").ap()

    ctx = ExitStack()
    with ctx:
        p = P(nc, ctx)
        sb = lambda name, shape, dt: ctx.enter_context(nc.sbuf_tensor(name, shape, dt))

        prm = sb("prm", [128, 13 * DC + 3 * FC], F32)
        prm_slots = {}
        off = 0
        for nm in ("pb_la", "g_la", "b_la", "b2_lf", "g_lf", "b_lf", "pb_sa", "g_sa",
                   "b_sa", "b2_pf", "g_pf", "b_pf", "b2_q"):
            prm_slots[nm] = off
            off += DC
        for nm in ("b1_lf", "b1_pf", "b1_q"):
            prm_slots[nm] = off
            off += FC
        id_sb = sb("id_sb", [128, 128], BF16)
        ones128 = sb("ones128s", [128, 128], BF16)
        ones_bf = sb("ones_bfs", [128, 1], BF16)
        ones_f = sb("ones_fs", [128, 1], F32)
        ones_fr = sb("ones_frs", [1, 128], F32)
        rows = sb("rows", [1, 2048], F32)
        nrm = sb("nrm", [1, 64], F32)
        vscr = sb("vscr", [1, 128], F32)
        negc = sb("negc", [128, 8], F32)
        linv = sb("linv", [128, 4], F32)
        invg = sb("invg", [128, 4], F32)
        qpart = sb("qpart", [128, 2], F32)
        qrow = sb("qrow", [1, 128], F32)
        qcol = sb("qcol", [128, 128], F32)
        rb = sb("rb", [128, 512], F32)
        mrb = sb("mrb", [128, 512], F32)
        tsc = [sb(f"tsc{i}", [128, 512], F32) for i in range(2)]
        fsc = [sb(f"fsc{i}", [128, 512], F32) for i in range(2)]
        bsc = [None, None] + [sb(f"bsc{i}", [128, 512], BF16) for i in (2, 3)]
        sqb = sb("sqb", [128, 4 * 512], BF16)

        ps_big = [ctx.enter_context(nc.psum_tensor(f"big{i}", [128, 512], F32)) for i in range(2)]
        ps_sc = [ctx.enter_context(nc.psum_tensor(f"sc{i}", [128, 512], F32)) for i in range(2)]
        ps_o = ctx.enter_context(nc.psum_tensor("ps_o", [128, 512], F32))
        ps_tr = ctx.enter_context(nc.psum_tensor("ps_tr", [128, 512], F32))
        ps_st = ctx.enter_context(nc.psum_tensor("ps_st", [128, 512], F32))
        ps_nm = ctx.enter_context(nc.psum_tensor("ps_nm", [128, 512], F32))

        def dma(eng, out, in_, waits=()):
            p.waits(eng, waits)
            return p.do(eng, lambda e, o=out, i=in_: e.dma_start(out=o, in_=i), mark=True, dma=True)

        def mm(out, lhsT, rhs, start, stop, waits=(), mark=False):
            p.waits("tensor", waits)
            return p.do("tensor", lambda e, o=out, l=lhsT, r=rhs, s1=start, s2=stop:
                        e.matmul(o, l, r, start=s1, stop=s2), mark=mark)

        def act(out, in_, func, bias=0.0, scale=1.0, waits=(), mark=True):
            p.waits("scalar", waits)
            return p.do("scalar", lambda e, o=out, i=in_, f=func, b=bias, s=scale:
                        e.activation(o, i, f, bias=b, scale=s), mark=mark)

        def vop(fn, waits=(), mark=True):
            p.waits("vector", waits)
            return p.do("vector", fn, mark=mark)

        def bar(eng, evs):
            p.waits(eng, evs)
            return p.mark_now(eng)

        _vs_last = [None]

        def vsync():
            ev = vop(lambda e: e.tensor_scalar(vscr[0:1, :], vscr[0:1, :],
                                               1.0, None, ALU.mult),
                     waits=[_vs_last[0]] if _vs_last[0] else [])
            _vs_last[0] = ev
            return ev

        def dma_drain(eng):
            evs = [(f"d{eng}{i}", p.cnt[f"d{eng}{i}"]) for i in range(P.NCH)
                   if p.cnt[f"d{eng}{i}"] > 0]
            return bar(eng, evs)

        def finish():
            for _e in ("sync", "scalar", "gpsimd"):
                for _i in range(P.NCH):
                    _s = f"d{_e}{_i}"
                    if p.cnt[_s] > 0:
                        p.wait(_e, (_s, p.cnt[_s]))
            with nc.Block() as block:
                p.replay(block)

        pidc = [None]
        def _getpid(e):
            pidc[0] = e.partition_id()
            return e.nop()
        p.do("scalar", _getpid)
        ev_init = []
        for nm in prm_slots:
            n = FC if nm.startswith("b1") else DC
            o = prm_slots[nm]
            ev_init.append(dma("sync", prm[:, o:o + n], din[nm]))
        for dst, src in [(id_sb, ident), (ones_bf, ones_bf_c), (ones_f, ones_f_c),
                         (ones_fr, ones_f_r), (ones128, ones128_c)]:
            ev_init.append(dma("sync", dst[:, :], src))
        ev_prm = bar("sync", ev_init)
        ev_negc0 = vop(lambda e: e.memset(negc[:, 0:8], -30.0))
        _vs_last[0] = vop(lambda e: e.memset(vscr[0:1, :], 0.0))

        ev_xw = [dma("gpsimd", dsc["x_win_bf"][:, 512 * s:512 * (s + 1)],
                     din["x_win_t"][:, 512 * s:512 * (s + 1)]) for s in range(NSL)]

        st_sq2 = [None, None]
        st_sqb = [[] ]
        war_rb = []
        war_st = [None]
        war_rows = [None, None]
        war_bc = [None, None]
        war_bcn = [None]

        def layernorm(src_evs, src, out_fn):
            ev_m1e = None
            ev_mmA = None
            ev_mmB = None
            last_sq = None
            wr0 = [w for w in war_rows if w]
            evA = None
            wst = war_st[0] or []
            for c in range(DC):
                xap = src(c)
                isbf = xap.tensor.dtype == BF16
                ev_m1e = mm(ps_st[0:1, :], (ones_bf if isbf else ones_f)[:, :], xap,
                            c == 0, c == DC - 1,
                            waits=[src_evs[c]] + (wst if c == 0 else []),
                            mark=(c == DC - 1))
            for c in range(DC):
                xap = src(c)
                slot = c % 4
                w = [src_evs[c]] + (st_sqb[0] if c == 0 else [])
                if c >= 4:
                    w.append(ev_mmA)
                vop(lambda e, xap=xap, slot=slot: e.tensor_tensor(
                    sqb[:, 512 * slot:512 * (slot + 1)], xap, xap, ALU.mult),
                    waits=w, mark=False)
                ev_s = vsync()
                last_sq = ev_s
                w2 = [ev_s] + (wst if c == 0 else [])
                if c == 4:
                    w2.append(evA)
                ev_m2 = mm(ps_nm[0:1, :], ones_bf[:, :], sqb[:, 512 * slot:512 * (slot + 1)],
                           c % 4 == 0, c % 4 == 3, waits=w2, mark=(c % 4 == 3))
                if c == 3:
                    ev_mmA = ev_m2
                    evA = act(rows[0:1, 512:1024], ps_nm[0:1, :], AF.Copy,
                              bias=0.0, scale=1.0 / D, waits=[ev_mmA] + wr0)
                elif c == 7:
                    ev_mmB = ev_m2
            st_sqb[0] = [ev_mmB, last_sq]
            wr = [w for w in war_rows if w]
            ev_ma = vop(lambda e: e.tensor_scalar(rows[0:1, 0:512], ps_st[0:1, :],
                                                  1.0 / D, None, ALU.mult),
                        waits=[ev_m1e] + wr)
            ev_r1 = vop(lambda e: e.scalar_tensor_tensor(
                rows[0:1, 512:1024], ps_nm[0:1, :], 1.0 / D, rows[0:1, 512:1024],
                ALU.mult, ALU.add), waits=[ev_mmB, evA])
            war_st[0] = [ev_ma, ev_r1]
            ev_c = vop(lambda e: e.tensor_tensor(rows[0:1, 1024:1536], rows[0:1, 0:512],
                                                 rows[0:1, 0:512], ALU.mult), waits=[ev_ma])
            ev_d = vop(lambda e: e.tensor_tensor(rows[0:1, 1024:1536], rows[0:1, 512:1024],
                                                 rows[0:1, 1024:1536], ALU.subtract),
                       waits=[ev_r1, ev_c])
            ev_e_ = vop(lambda e: e.tensor_scalar(rows[0:1, 1024:1536], rows[0:1, 1024:1536],
                                                  EPS, None, ALU.add), waits=[ev_d])
            ev_var = vop(lambda e: e.tensor_scalar(rows[0:1, 1024:1536], rows[0:1, 1024:1536],
                                                   1e-30, None, ALU.max), waits=[ev_e_])
            ev_ln = act(rows[0:1, 512:1024], rows[0:1, 1024:1536], AF.Ln, waits=[ev_var])
            ev_rst = act(rows[0:1, 1536:2048], rows[0:1, 512:1024], AF.Exp,
                         bias=0.0, scale=-0.5, waits=[ev_ln])
            ev_mr = vop(lambda e: e.scalar_tensor_tensor(rows[0:1, 1024:1536], rows[0:1, 0:512],
                                                         1.0, rows[0:1, 1536:2048],
                                                         ALU.mult, ALU.mult),
                        waits=[ev_rst, ev_ma, ev_var])
            wb = [w for w in war_rb if w]
            ev_w1 = dma("sync", dsc["bc_ln"][0:1, 0:512], rows[0:1, 1536:2048],
                        waits=[ev_rst] + ([war_bc[0]] if war_bc[0] else []))
            ev_w2 = dma("sync", dsc["bc_ln"][0:1, 512:1024], rows[0:1, 1024:1536],
                        waits=[ev_mr] + ([war_bc[1]] if war_bc[1] else []))
            ev_rb = dma("sync", rb[:, :], dsc["bc_ln"][0:1, 0:512].broadcast_to([128, 512]),
                        waits=[ev_w1] + wb)
            ev_mrb = dma("sync", mrb[:, :], dsc["bc_ln"][0:1, 512:1024].broadcast_to([128, 512]),
                         waits=[ev_w2] + wb)
            war_rows[:] = [ev_w1, ev_w2]
            war_bc[0], war_bc[1] = ev_rb, ev_mrb
            ev_bc = bar("sync", [ev_rb, ev_mrb])
            evs = [out_fn(c, ev_bc, ev_bc) for c in range(DC)]
            war_rb[:] = [evs[-1]]
            return evs

        def norm_shift(qslab, kslab, ev_q, ev_k, war_negc):
            ev_red = [None] * 16
            for idx in range(16):
                which, t = ("q", idx) if idx < 8 else ("k", idx - 8)
                slab = qslab if which == "q" else kslab
                evs = ev_q if which == "q" else ev_k
                par = idx % 2
                bank = ps_st if par == 0 else ps_nm
                ev_s = act(bsc[2 + par][:, :], slab[:, 512 * t:512 * (t + 1)], AF.Square,
                           waits=[evs[t]] + (st_sq2[par] or []))
                w = [ev_s]
                if idx < 2 and war_st[0]:
                    w.extend(war_st[0])
                if idx >= 2:
                    w.append(ev_red[idx - 2])
                ev_m = mm(bank[:, :], ones128[:, :], bsc[2 + par][:, :], True, True,
                          waits=w, mark=True)
                st_sq2[par] = [ev_m, ev_s]
                ev_red[idx] = vop(lambda e, bank=bank, idx=idx: e.reduce_max(
                    nrm[0:1, idx:idx + 1], bank[0:1, :], axis=AX.X), waits=[ev_m])
            ev1 = ev_red[14]
            ev2 = ev_red[15]
            war_st[0] = [ev2]
            ev3 = vop(lambda e: e.tensor_tensor(nrm[0:1, 16:24], nrm[0:1, 0:8],
                                                nrm[0:1, 8:16], ALU.mult), waits=[ev1, ev2])
            ev4 = act(nrm[0:1, 24:32], nrm[0:1, 16:24], AF.Sqrt, waits=[ev3])
            ev5 = vop(lambda e: e.tensor_scalar(nrm[0:1, 32:40], nrm[0:1, 24:32],
                                                -0.125, None, ALU.mult), waits=[ev4])
            ev_w = dma("sync", dsc["bc_nc"], nrm[0:1, 32:40],
                       waits=[ev5] + ([war_bcn[0]] if war_bcn[0] else []))
            ev6 = dma("sync", negc[:, 0:8], dsc["bc_nc"].broadcast_to([128, 8]),
                      waits=[ev_w] + ([war_negc] if war_negc else []))
            ev6 = dma("sync", negc[:, 0:8], dsc["bc_nc"].broadcast_to([128, 8]),
                      waits=[ev6])
            war_bcn[0] = ev6
            war_st[0] = [ev2]
            return bar("sync", [ev6])

        def mha_stage(ph):
            qslab, kslab, vslab = ph["qs"], ph["ks"], ph["vs"]
            esl_b = ph["esl"]
            war = dict(qk=[None, None], v=None, ab=[None] * 4, oslab=None,
                       ot=None, ymha=None, xbf=None, e=[None, None], negc=ev_negc0,
                       edump=[None, None], dbg=[], xmbf_dma=[])
            sp_evs = [[None] * DC for _ in range(NSL)]
            ev_vones = vop(lambda e: e.memset(
                vslab.rearrange("p (s h o) -> p s h o", s=4, h=H)[:, :, :, 64:65], 1.0),
                waits=ph.get("fence", []))

            for s in range(NSL):
                par = s % 2
                esl_t = esl_b[par]

                def esl(c, t=esl_t):
                    return t[:, 512 * c:512 * (c + 1)]

                if ph["phase"] == 1:
                    ev_x = []
                    for c in range(DC):
                        ev_x.append(dma("sync", ph["xbf"][:, 512 * c:512 * (c + 1)],
                                        dsc["x_win_bf"][128 * c:128 * (c + 1), 512 * s:512 * (s + 1)],
                                        waits=[ev_xw[s]] + ([war["xbf"]] if war["xbf"] and c == 0 else [])))
                    ev_x = [bar("sync", [e]) for e in ev_x]

                    def ln0_out(c, ev_rb, ev_mrb):
                        ev_p1 = vop(lambda e, c=c: e.tensor_tensor(
                            tsc[c % 2][:, :], ph["xbf"][:, 512 * c:512 * (c + 1)], rb[:, :],
                            ALU.mult), waits=[ev_rb] + list(war["dbg"]), mark=True)
                        if "dbg_rb" in dsc and s == _DEBUG.get("dbg_s", 0) and c == 0 and ph["phase"] == 1:
                            war["dbg"].append(dma("sync", dsc["dbg_y"][:, 0:512], tsc[0][:, :],
                                                  waits=[ev_p1]))
                        w = [ev_p1, ev_mrb] + ([war["e"][par]] if war["e"][par] and c == 0 else [])
                        if war["edump"][par] and c == 0:
                            w.append(war["edump"][par])
                        vop(lambda e, dst=esl(c), c=c: e.tensor_tensor(
                            dst, tsc[c % 2][:, :], mrb[:, :], ALU.subtract), waits=w,
                            mark=False)
                        return vsync()
                    if "dbg_x" in dsc and s == _DEBUG.get("dbg_s", 0) and ph["phase"] == 1:
                        dma("sync", dsc["dbg_x"], ph["xbf"][:, :], waits=ev_x)
                    ev_e = layernorm(ev_x, lambda c: ph["xbf"][:, 512 * c:512 * (c + 1)], ln0_out)
                    war["xbf"] = ev_e[-1]
                    if "dbg_rows" in dsc and s == _DEBUG.get("dbg_s", 0) and ph["phase"] == 1:
                        war_rows.append(dma("sync", dsc["dbg_rows"], rows[0:1, :], waits=[ev_e[0]]))
                        war_rb.append(dma("sync", dsc["dbg_rb"][:, 0:512], rb[:, :], waits=[ev_e[0]]))
                        war_rb.append(dma("sync", dsc["dbg_rb"][:, 512:1024], mrb[:, :], waits=[ev_e[0]]))
                    if "dbg_e" in dsc:
                        for c in range(DC):
                            war["edump"][par] = dma(
                                "sync", dsc["dbg_e"][128 * c:128 * (c + 1), 512 * s:512 * (s + 1)],
                                esl(c), waits=[ev_e[c]])
                else:
                    ev_e = []
                    for c in range(DC):
                        w = [ph["ev_e2tm"]] + ([war["e"][par]] if war["e"][par] and c == 0 else [])
                        p.waits("scalar", w)
                        ev_e.append(p.do("scalar", lambda e, dst=esl(c), c=c, s=s: e.dma_start_transpose(
                            dst, dsc["e2tm"][512 * s:512 * (s + 1), 128 * c:128 * (c + 1)]),
                            mark=True, dma=True))
                    ev_e = [bar("scalar", [e]) for e in ev_e]

                ev_q = [None] * 8
                ev_k = [None] * 8
                ev_bigfree = [None, None]
                ev_vlast = None
                bigpar = 0
                for kind, t in [("q", t) for t in range(8)] + [("k", t) for t in range(8)] + \
                               [("v", t) for t in range(8)]:
                    bank = ps_big[bigpar]
                    wslab = ph["w" + kind]
                    ev_g = None
                    for c in range(DC):
                        if kind == "v":
                            g, half = t // 2, t % 2
                            lhs = esl(c)[:, 128 * g:128 * (g + 1)]
                            rhs = wslab[:, 1024 * c + 512 * half:1024 * c + 512 * (half + 1)]
                        else:
                            lhs = wslab[:, 1024 * c + 128 * t:1024 * c + 128 * (t + 1)]
                            rhs = esl(c)
                        w = [ph["ev_w"], ev_e[c]]
                        if c == 0 and ev_bigfree[bigpar] is not None:
                            w.append(ev_bigfree[bigpar])
                        if war["dbg"]:
                            w.extend(war["dbg"])
                        ev_g = mm(bank[:, :], lhs, rhs, c == 0, c == DC - 1, waits=w,
                                  mark=(c == DC - 1))
                    if kind == "q":
                        ev_q[t] = act(qslab[par][:, 512 * t:512 * (t + 1)], bank[:, :], AF.Copy,
                                      waits=[ev_g] + ([war["qk"][par]] if war["qk"][par] else []))
                        ev_bigfree[bigpar] = ev_q[t]
                        if _DEBUG.get("trunc") == "q0" and t == 0 and ph["phase"] == 1:
                            dma("sync", dsc["dbg_q"][:, 0:512], qslab[par][:, 0:512], waits=[ev_q[0]])
                            return None
                    elif kind == "k":
                        ev_k[t] = act(kslab[par][:, 512 * t:512 * (t + 1)], bank[:, :], AF.Copy,
                                      waits=[ev_g])
                        ev_bigfree[bigpar] = ev_k[t]
                    else:
                        g, half = t // 2, t % 2
                        dst = vslab.rearrange("p (s h o) -> p s h o", s=4, h=H)[
                            :, g, 8 * half:8 * (half + 1), 0:64]
                        srcv = bank.rearrange("p (h o) -> p h o", h=8)
                        w = [ev_g, ev_vones] + ([war["v"]] if war["v"] else [])
                        ev_vlast = act(dst, srcv, AF.Copy, waits=w)
                        ev_bigfree[bigpar] = ev_vlast
                    bigpar ^= 1
                war["qk"][par] = None
                war["v"] = None

                ev_negc = norm_shift(qslab[par], kslab[par], ev_q, ev_k, war["negc"])
                if "dbg_q" in dsc and s == _DEBUG.get("dbg_s", 0) and ph["phase"] == 1:
                    war["dbg"].append(dma("sync", dsc["dbg_q"], qslab[par][:, :], waits=ev_q))
                    war["dbg"].append(dma("sync", dsc["dbg_k"], kslab[par][:, :], waits=ev_k))
                    war["dbg"].append(dma("sync", dsc["dbg_negc"], negc[:, 0:8], waits=[ev_negc]))
                if _DEBUG.get("trunc") == "qkv" and ph["phase"] == 1:
                    return None

                ev_heads = []
                for h in range(H):
                    pair, half = h // 2, h % 2
                    ob = ps_o if h % 2 == 0 else ps_tr
                    ev_exp = [None] * 4
                    if ph["phase"] == 1:
                        scb = ps_sc[h % 2]
                        ev_sc = None
                        for chk in range(4):
                            lhs = kslab[par][64 * half:64 * half + 64,
                                             512 * pair + 128 * chk:512 * pair + 128 * (chk + 1)]
                            rhs = qslab[par][64 * half:64 * half + 64,
                                             512 * pair + 128 * chk:512 * pair + 128 * (chk + 1)]
                            w = [ev_q[pair], ev_k[pair]]
                            if len(ev_heads) >= 2:
                                w.append(ev_heads[-2])
                            ev_sc = mm(scb[:, 128 * chk:128 * (chk + 1)], lhs, rhs, True, True,
                                       waits=w, mark=(chk == 3))
                        ev_e1 = act(ph["abuf"][:, 0:512], scb[:, :], AF.Exp,
                                    bias=negc[:, pair:pair + 1], scale=0.125,
                                    waits=[ev_sc, ev_negc] + ([war["ab"][0]] if war["ab"][0] else []))
                        vop(lambda e: e.tensor_tensor(ph["abuf"][:, 0:512],
                                                      ph["abuf"][:, 0:512],
                                                      ph["msk"][:, :], ALU.mult),
                            waits=[ev_e1, ph["ev_msk"]], mark=False)
                        ev_msk = vsync()
                        ev_exp = [ev_msk] * 4
                        n_ch = 1
                    else:
                        for chk in range(4):
                            lhs = kslab[par][64 * half:64 * half + 64,
                                             512 * pair + 128 * chk:512 * pair + 128 * (chk + 1)]
                            rhs = qslab[par][64 * half:64 * half + 64, 512 * pair:512 * (pair + 1)]
                            w = [ev_q[pair], ev_k[pair]]
                            if len(ev_heads) >= 2:
                                w.append(ev_heads[-2])
                            if chk >= 2 and ev_exp[chk - 2] is not None:
                                w.append(ev_exp[chk - 2])
                            ev_mm_sc = mm(ps_sc[chk % 2][:, :], lhs, rhs, True, True,
                                          waits=w, mark=True)
                            ev_exp[chk] = act(ph["abuf"][:, 512 * chk:512 * (chk + 1)],
                                              ps_sc[chk % 2][:, :], AF.Exp,
                                              bias=negc[:, pair:pair + 1], scale=0.125,
                                              waits=[ev_mm_sc, ev_negc] +
                                                    ([war["ab"][chk]] if war["ab"][chk] else []))
                        n_ch = 4
                    ev_av = None
                    for qc in range(4):
                        for chk in range(n_ch):
                            if ph["phase"] == 1:
                                lhs = ph["abuf"][:, 128 * qc:128 * (qc + 1)]
                                vsub = qc
                            else:
                                lhs = ph["abuf"][:, 512 * chk + 128 * qc:512 * chk + 128 * (qc + 1)]
                                vsub = chk
                            rhs = vslab[:, 1040 * vsub + 65 * h:1040 * vsub + 65 * (h + 1)]
                            ev_av = mm(ob[:, 65 * qc:65 * (qc + 1)], lhs, rhs, chk == 0,
                                       chk == n_ch - 1, waits=[ev_exp[chk], ev_vlast],
                                       mark=(chk == n_ch - 1 and qc == 3))
                    ev_rec = [vop(lambda e, qc=qc, ob=ob: e.reciprocal(
                        linv[:, qc:qc + 1], ob[:, 65 * qc + 64:65 * qc + 65]),
                        waits=[ev_av]) for qc in range(4)]
                    ev_stt = None
                    for qc in range(4):
                        vv = vslab[:, 1040 * qc + 65 * h:1040 * qc + 65 * h + 64]
                        dst = ph["oslab"][:, 1024 * qc + 64 * h:1024 * qc + 64 * (h + 1)]
                        w = [ev_rec[qc]] + ([war["oslab"]] if war["oslab"] and h == 0 else [])
                        ev_stt = vop(lambda e, dst=dst, ob=ob, qc=qc, vv=vv:
                                     e.scalar_tensor_tensor(dst, ob[:, 65 * qc:65 * qc + 64],
                                                            linv[:, qc:qc + 1], vv,
                                                            ALU.mult, ALU.subtract), waits=w)
                    ev_heads.append(ev_stt)
                    war["ab"] = [ev_stt] * 4
                war["oslab"] = None
                ev_attn = vsync()
                war["negc"] = ev_attn
                if "dbg_o" in dsc and s == _DEBUG.get("dbg_s", 0) and ph["phase"] == 1:
                    war["dbg"].append(dma("sync", dsc["dbg_v"], vslab[:, :], waits=[ev_attn]))
                    war["dbg"].append(dma("sync", dsc["dbg_o"], ph["oslab"][:, :], waits=[ev_attn]))
                    war["dbg"].append(dma("sync", dsc["dbg_a"], ph["abuf"][:, 0:512], waits=[ev_attn]))

                ev_ot = []
                ev_tr = None
                for k in range(8):
                    for qc in range(4):
                        w = [ev_attn, ev_prm]
                        if war["ot"] and k == 0 and qc == 0:
                            w.append(war["ot"])
                        if k > 0 and qc == 0:
                            w.append(ev_ot[k - 1])
                        ev_tr = mm(ps_tr[:, 128 * qc:128 * (qc + 1)],
                                   ph["oslab"][:, 1024 * qc + 128 * k:1024 * qc + 128 * (k + 1)],
                                   id_sb[:, :], True, True, waits=w, mark=(qc == 3))
                    ev_ot.append(act(ph["ot"][:, 512 * k:512 * (k + 1)], ps_tr[:, :], AF.Copy,
                                     waits=[ev_tr]))
                war["ot"] = None

                ev_proj = []
                ev_pm_last = None
                for c in range(DC):
                    bank = ps_big[c % 2]
                    ev_pm = None
                    for k in range(8):
                        w = [ev_ot[k], ph["ev_w"]]
                        if k == 0 and ev_bigfree[c % 2] is not None:
                            w.append(ev_bigfree[c % 2])
                        ev_pm = mm(bank[:, :],
                                   ph["wp"][:, 1024 * k + 128 * c:1024 * k + 128 * (c + 1)],
                                   ph["ot"][:, 512 * k:512 * (k + 1)], k == 0, k == 7,
                                   waits=w, mark=(k == 7))
                    ev_pm_last = ev_pm
                    ev_ev = act(ph["ymha"][:, 512 * c:512 * (c + 1)], bank[:, :], AF.Identity,
                                bias=prm[:, prm_slots[ph["pb"]] + c:prm_slots[ph["pb"]] + c + 1],
                                waits=[ev_pm, ev_prm] + ([war["ymha"]] if war["ymha"] and c == 0 else []))
                    ev_bigfree[c % 2] = ev_ev
                    ev_proj.append(vop(lambda e, c=c, src2=esl(c): e.tensor_tensor(
                        ph["ymha"][:, 512 * c:512 * (c + 1)], ph["ymha"][:, 512 * c:512 * (c + 1)],
                        src2, ALU.add), waits=[ev_ev]))
                war["ymha"] = None

                def mha_ln_out(c, ev_rb, ev_mrb):
                    ev1 = vop(lambda e, c=c: e.tensor_tensor(
                        tsc[c % 2][:, :], ph["ymha"][:, 512 * c:512 * (c + 1)], rb[:, :],
                        ALU.mult), waits=[ev_rb])
                    ev2 = vop(lambda e, c=c: e.tensor_tensor(
                        tsc[c % 2][:, :], tsc[c % 2][:, :], mrb[:, :], ALU.subtract),
                        waits=[ev1, ev_mrb])
                    w3 = [ev2] + (war["xmbf_dma"] if c == 0 else [])
                    return vop(lambda e, c=c: e.tensor_scalar(
                        ph["xmbf"][:, 512 * c:512 * (c + 1)], tsc[c % 2][:, :],
                        prm[:, prm_slots[ph["g"]] + c:prm_slots[ph["g"]] + c + 1],
                        prm[:, prm_slots[ph["b"]] + c:prm_slots[ph["b"]] + c + 1],
                        ALU.mult, ALU.add), waits=w3)
                if "dbg_y" in dsc and s == _DEBUG.get("dbg_s", 0) and ph["phase"] == 1:
                    war["dbg"].append(dma("sync", dsc["dbg_y"], ph["ymha"][:, :], waits=ev_proj))
                evs2 = layernorm(ev_proj, lambda c: ph["ymha"][:, 512 * c:512 * (c + 1)],
                                 mha_ln_out)
                if "dbg_xm" in dsc and s == _DEBUG.get("dbg_s", 0) and ph["phase"] == 1:
                    war["dbg"].append(dma("sync", dsc["dbg_xm"], ph["xmbf"][:, :], waits=list(evs2)))
                    _d1 = dma("sync", dsc["dbg_mrb"][:, 0:512], rb[:, :], waits=[evs2[0]])
                    _d2 = dma("sync", dsc["dbg_mrb"][:, 512:1024], mrb[:, :], waits=[evs2[0]])
                    _d3 = dma("sync", dsc["dbg_mrows"], rows[0:1, :], waits=[evs2[0]])
                    war_rb.extend([_d1, _d2])
                    war_rows.append(_d3)
                    war["dbg"].extend([_d1, _d2, _d3])
                for c in range(DC):
                    sp_evs[s][c] = dma("sync", ph["spill"][128 * c:128 * (c + 1), 512 * s:512 * (s + 1)],
                                       ph["xmbf"][:, 512 * c:512 * (c + 1)], waits=[evs2[c]])
                war["qk"][par] = ev_attn
                war["v"] = ev_attn
                war["oslab"] = ev_ot[-1]
                war["ot"] = ev_pm_last
                war["ymha"] = evs2[-1]
                war["e"][par] = ev_proj[-1]
                war["xmbf_dma"] = list(sp_evs[s])
            return sp_evs

        def ffn_stage(ph):
            war = dict(xin=None, h=[None, None], u=None, ubf=[], fsc=[None, None])
            ev_last = None
            inplace = ph["ubf"] is ph["u"]
            for s in range(NSL):
                ev_x = []
                for c in range(DC):
                    ev_x.append(dma("sync", ph["xin"][:, 512 * c:512 * (c + 1)],
                                    ph["spill"][128 * c:128 * (c + 1), 512 * s:512 * (s + 1)],
                                    waits=[ph["sp_evs"][s][c]] + ([war["xin"]] if war["xin"] and c == 0 else [])))
                ev_x = [bar("sync", [e]) for e in ev_x]
                dbgf = "dbg_f_x" in dsc and s == 0 and ph.get("dbgf")
                if dbgf:
                    dma("sync", dsc["dbg_f_x"], ph["xin"][:, :], waits=list(ev_x))
                ev_evac = [None] * DC
                ev_wm_last = None
                for halfI in range(2):
                    ev_h = [None] * 16
                    ev_bigfree = [None, None]
                    for f16 in range(16):
                        fc = 16 * halfI + f16
                        bank = ps_big[f16 % 2]
                        ev_m = None
                        for c in range(DC):
                            w = [ph["ev_w1"], ev_x[c]]
                            if c == 0 and ev_bigfree[f16 % 2] is not None:
                                w.append(ev_bigfree[f16 % 2])
                            if c == 0 and f16 == 0 and war["h"][halfI] is not None:
                                w.append(war["h"][halfI])
                            ev_m = mm(bank[:, :],
                                      ph["w1"][:, 4096 * c + 128 * fc:4096 * c + 128 * (fc + 1)],
                                      ph["xin"][:, 512 * c:512 * (c + 1)], c == 0, c == DC - 1,
                                      waits=w, mark=(c == DC - 1))
                        ev_h[f16] = act(ph["h"][:, 512 * f16:512 * (f16 + 1)], bank[:, :], AF.Relu,
                                        bias=prm[:, prm_slots[ph["b1"]] + fc:prm_slots[ph["b1"]] + fc + 1],
                                        waits=[ev_m, ev_prm])
                        ev_bigfree[f16 % 2] = ev_h[f16]
                    if halfI == 0:
                        ev_scfree = [None, None]
                    else:
                        ev_scfree = [ev_evac[DC - 2], ev_evac[DC - 1]]
                    for c in range(DC):
                        bank = ps_sc[c % 2]
                        ev_m = None
                        for f16 in range(16):
                            fc = 16 * halfI + f16
                            w = [ph["ev_w2"], ev_h[f16]]
                            if f16 == 0 and ev_scfree[c % 2] is not None:
                                w.append(ev_scfree[c % 2])
                            ev_m = mm(bank[:, :],
                                      ph["w2"][:, 1024 * fc + 128 * c:1024 * fc + 128 * (c + 1)],
                                      ph["h"][:, 512 * f16:512 * (f16 + 1)], f16 == 0, f16 == 15,
                                      waits=w, mark=(f16 == 15))
                        ev_wm_last = ev_m
                        if halfI == 0:
                            wu = [ev_m, ev_prm]
                            if c == 0 and war["u"]:
                                wu.append(war["u"])
                            if c == 0 and war.get("u_dma"):
                                wu.extend(war["u_dma"])
                            ev_evac[c] = act(ph["u"][:, 512 * c:512 * (c + 1)], bank[:, :], AF.Identity,
                                             bias=prm[:, prm_slots[ph["b2"]] + c:prm_slots[ph["b2"]] + c + 1],
                                             waits=wu)
                            ev_scfree[c % 2] = ev_evac[c]
                        else:
                            evf = act(fsc[c % 2][:, :], bank[:, :], AF.Copy,
                                      waits=[ev_m] + ([war["fsc"][c % 2]] if war["fsc"][c % 2] else []))
                            ev_ad = vop(lambda e, c=c: e.tensor_tensor(
                                ph["u"][:, 512 * c:512 * (c + 1)], ph["u"][:, 512 * c:512 * (c + 1)],
                                fsc[c % 2][:, :], ALU.add), waits=[evf, ev_evac[c]])
                            war["fsc"][c % 2] = ev_ad
                            ev_evac[c] = ev_ad
                            ev_scfree[c % 2] = evf
                    war["h"] = [ev_wm_last, ev_wm_last]
                if dbgf:
                    dma("sync", dsc["dbg_f_h"][:, 0:4096], ph["h"][:, 0:4096],
                        waits=[ev_evac[0], ev_evac[1]])
                ev_y = []
                for c in range(DC):
                    ev_y.append(vop(lambda e, c=c: e.tensor_tensor(
                        ph["u"][:, 512 * c:512 * (c + 1)], ph["u"][:, 512 * c:512 * (c + 1)],
                        ph["xin"][:, 512 * c:512 * (c + 1)], ALU.add), waits=[ev_evac[c]]))
                war["xin"] = ev_y[-1]
                if dbgf:
                    dma("sync", dsc["dbg_f_u"], ph["u"][:, :], waits=list(ev_y))

                def ffn_ln_out(c, ev_rb, ev_mrb):
                    ev1 = vop(lambda e, c=c: e.tensor_tensor(
                        tsc[c % 2][:, :], ph["u"][:, 512 * c:512 * (c + 1)], rb[:, :], ALU.mult),
                        waits=[ev_rb])
                    ev2 = vop(lambda e, c=c: e.tensor_tensor(
                        tsc[c % 2][:, :], tsc[c % 2][:, :], mrb[:, :], ALU.subtract),
                        waits=[ev1, ev_mrb])
                    w = [ev2] + (war["ubf"] if c == 0 else [])
                    ev3 = vop(lambda e, c=c: e.tensor_scalar(
                        ph["ubf"][:, 512 * c:512 * (c + 1)], tsc[c % 2][:, :],
                        prm[:, prm_slots[ph["g"]] + c:prm_slots[ph["g"]] + c + 1],
                        prm[:, prm_slots[ph["b"]] + c:prm_slots[ph["b"]] + c + 1],
                        ALU.mult, ALU.add), waits=w, mark=not ph.get("post_pe"))
                    return vsync() if ph.get("post_pe") else ev3
                evs = layernorm(ev_y, lambda c: ph["u"][:, 512 * c:512 * (c + 1)], ffn_ln_out)
                war["u"] = evs[-1]
                war["ubf"] = list(ph["post"](s, evs))
                if inplace:
                    war["u"] = None
                    war["u_dma"] = war["ubf"]
                ev_last = war["ubf"]
            return ev_last

        p.stage_tag = "s1"
        with ExitStack() as _sctx:
            wq_s = _sctx.enter_context(nc.sbuf_tensor("wq1s", [128, DC * 1024], BF16))
            wk_s = _sctx.enter_context(nc.sbuf_tensor("wk1s", [128, DC * 1024], BF16))
            wv_s = _sctx.enter_context(nc.sbuf_tensor("wv1s", [128, DC * 1024], BF16))
            pw_s = _sctx.enter_context(nc.sbuf_tensor("pw1s", [128, DC * 1024], BF16))
            xbf1 = _sctx.enter_context(nc.sbuf_tensor("xbf1", [128, DC * 512], BF16))
            esl0 = _sctx.enter_context(nc.sbuf_tensor("esl0", [128, 4096], BF16))
            esl1 = _sctx.enter_context(nc.sbuf_tensor("esl1", [128, 4096], BF16))
            qs0 = _sctx.enter_context(nc.sbuf_tensor("qsl0", [128, 8 * 512], BF16))
            qs1 = _sctx.enter_context(nc.sbuf_tensor("qsl1", [128, 8 * 512], BF16))
            ks0 = _sctx.enter_context(nc.sbuf_tensor("ksl0", [128, 8 * 512], BF16))
            ks1 = _sctx.enter_context(nc.sbuf_tensor("ksl1", [128, 8 * 512], BF16))
            vs0 = _sctx.enter_context(nc.sbuf_tensor("vsl", [128, 4 * 1040], BF16))
            oslab1 = _sctx.enter_context(nc.sbuf_tensor("osl1", [128, 4096], BF16))
            ot1 = _sctx.enter_context(nc.sbuf_tensor("ot1", [128, 4096], BF16))
            ymha1 = _sctx.enter_context(nc.sbuf_tensor("ym1", [128, 4096], F32))
            xmbf1 = _sctx.enter_context(nc.sbuf_tensor("xmb1", [128, 4096], BF16))
            abuf1 = _sctx.enter_context(nc.sbuf_tensor("ab1", [128, 512], BF16))
            msk1 = _sctx.enter_context(nc.sbuf_tensor("msk1", [128, 512], BF16))
            qstA0 = _sctx.enter_context(nc.sbuf_tensor("qstA0", [128, 512], F32))
            qstA1 = _sctx.enter_context(nc.sbuf_tensor("qstA1", [128, 512], F32))
            _evm0 = dma("sync", msk1[:, :], mask512)
            _evm0 = dma("sync", msk1[:, 448:512], mask512[:, 448:512], waits=[_evm0])
            ev_msk = bar("sync", [_evm0])
            ev_wl = []
            for slab, wsrc in [(wq_s, din["wq_la"]), (wk_s, din["wk_la"]),
                               (wv_s, din["wv_la"]), (pw_s, din["pw_la"])]:
                for c in range(DC):
                    ev_wl.append(dma("gpsimd", slab[:, 1024 * c:1024 * (c + 1)],
                                     wsrc[128 * c:128 * (c + 1), :]))
            ev_w = bar("gpsimd", ev_wl)
            ev_pc = {}
            for nm in ("w1_lf", "w2_lf", "wq_sa", "wk_sa", "wv_sa", "pw_sa", "w1_pf", "w2_pf"):
                ev_pc[nm] = dma("gpsimd", dsc[nm + "_bf"], din[nm])

            if "dbg_w" in dsc:
                dma("sync", dsc["dbg_w"], wq_s[:, :], waits=[ev_w])
            p.stage_tag = "qA"
            ev_qA = _quant_passA(p, din, [qstA0, qstA1], qcol, dma, act, vop, bar, [])
            p.stage_tag = "s1"
            ph1 = dict(phase=1, wq=wq_s, wk=wk_s, wv=wv_s, wp=pw_s, ev_w=ev_w,
                       qs=[qs0, qs1], ks=[ks0, ks1], vs=vs0, esl=[esl0, esl1],
                       oslab=oslab1, ot=ot1, ymha=ymha1, xmbf=xmbf1, xbf=xbf1,
                       abuf=abuf1, msk=msk1, ev_msk=ev_msk,
                       pb="pb_la", g="g_la", b="b_la", spill=dsc["xm1"])
            ev_sp1 = mha_stage(ph1)
            fence_s1 = [p.last_ev[e] for e in ("tensor", "vector", "scalar")
                        if p.last_ev[e]] + [dma_drain("sync")]

        if upto == "s1":
            finish()
            return nc

        p.stage_tag = "s2"
        with ExitStack() as _sctx:
            w1s = _sctx.enter_context(nc.sbuf_tensor("w1s2", [128, DC * 4096], BF16))
            w2s = _sctx.enter_context(nc.sbuf_tensor("w2s2", [128, FC * 1024], BF16))
            xin2 = _sctx.enter_context(nc.sbuf_tensor("xin2", [128, 4096], BF16))
            h2 = _sctx.enter_context(nc.sbuf_tensor("h2", [128, 16 * 512], BF16))
            usl2 = _sctx.enter_context(nc.sbuf_tensor("u2sl", [128, 4096], F32))
            ubf2 = _sctx.enter_context(nc.sbuf_tensor("ubf2", [128, 4096], BF16))
            ccs0 = _sctx.enter_context(nc.sbuf_tensor("ccs0", [128, 512], BF16))
            ccs1 = _sctx.enter_context(nc.sbuf_tensor("ccs1", [128, 512], BF16))
            evl = [dma("sync", w1s[:, 4096 * c:4096 * (c + 1)],
                       dsc["w1_lf_bf"][128 * c:128 * (c + 1), :],
                       waits=[ev_pc["w1_lf"]] + fence_s1) for c in range(DC)]
            ev_w1 = bar("sync", evl)
            evl = [dma("sync", w2s[:, 1024 * fc:1024 * (fc + 1)],
                       dsc["w2_lf_bf"][128 * fc:128 * (fc + 1), :],
                       waits=[ev_pc["w2_lf"]]) for fc in range(FC)]
            ev_w2 = bar("sync", evl)

            war_cc = [None, None, None]
            def post_lf(s, evs):
                last = []
                for tsub in range(4):
                    ev_tra = ev_trb = None
                    for c in range(8):
                        bank = ps_o if c < 4 else ps_tr
                        co = 128 * (c % 4)
                        w = [evs[c], ev_prm]
                        if war_cc[2] and c == 0:
                            w.append(war_cc[2])
                        ev_t = mm(bank[:, co:co + 128],
                                  ubf2[:, 512 * c + 128 * tsub:512 * c + 128 * (tsub + 1)],
                                  id_sb[:, :], True, True, waits=w, mark=(c % 4 == 3))
                        if c == 3:
                            ev_tra = ev_t
                        elif c == 7:
                            ev_trb = ev_t
                    vop(lambda e: e.tensor_copy(ccs0[:, :], ps_o[:, :]),
                        waits=[ev_tra] + ([war_cc[0]] if war_cc[0] else []), mark=False)
                    eva = vsync()
                    vop(lambda e: e.tensor_copy(ccs1[:, :], ps_tr[:, :]),
                        waits=[ev_trb] + ([war_cc[1]] if war_cc[1] else []), mark=False)
                    evb = vsync()
                    r0 = 512 * s + 128 * tsub
                    da = dma("sync", dsc["cc_in"][r0:r0 + 128, 0:512], ccs0[:, :], waits=[eva])
                    db = dma("sync", dsc["cc_in"][r0:r0 + 128, 512:1024], ccs1[:, :], waits=[evb])
                    war_cc[0], war_cc[1], war_cc[2] = da, db, evb
                    last = [da, db]
                return last

            ph2 = dict(w1=w1s, w2=w2s, ev_w1=ev_w1, ev_w2=ev_w2, xin=xin2, h=h2, u=usl2,
                       ubf=ubf2, b1="b1_lf", b2="b2_lf", g="g_lf", b="b_lf",
                       spill=dsc["xm1"], sp_evs=ev_sp1, post=post_lf, post_pe=True)
            ffn_stage(ph2)
            fence_s2 = [p.last_ev[e] for e in ("tensor", "vector", "scalar")
                        if p.last_ev[e]] + [dma_drain("sync")]

        if upto == "s2":
            finish()
            return nc

        p.stage_tag = "ex"
        p.waits("gpsimd", fence_s2)
        ev_cc = p.do("gpsimd", lambda e: e.collective_compute(
            "AllGather", ALU.bypass, replica_groups=[list(range(NCORES))],
            ins=[dsc["cc_in"].tensor.ap().opt()], outs=[dsc["cc_out"].tensor.ap().opt()]),
            cc=True)
        if _DEBUG.get("spill_out"):
            dsc["dbg_ccin"] = nc.dram_tensor("dbg_ccin", [T, D], BF16, kind="ExternalOutput").ap()
            dma("sync", dsc["dbg_ccin"], dsc["cc_in"], waits=[ev_cc])

        with ExitStack() as _sctx:
            gth = _sctx.enter_context(nc.sbuf_tensor("gth", [128, 16384], BF16))
            ev_g = []
            for j in range(16):
                il, qq = j // 4, j % 4
                def g_dma(e, il=il, qq=qq, j=j):
                    v = dsc["cc_out"].rearrange("(m i) x -> i m x", i=32)
                    vg = v[ds(4 * pidc[0], 4)]
                    return e.dma_start(out=gth[:, 1024 * j:1024 * (j + 1)],
                                       in_=vg[il, ds(128 * qq, 128)])
                p.waits("scalar", [ev_cc])
                ev_g.append(p.do("scalar", g_dma, mark=True, dma=True))
            ev_e2tm_l = [dma("scalar", dsc["e2tm"][128 * j:128 * (j + 1), :],
                            gth[:, 1024 * j:1024 * (j + 1)], waits=[ev_g[j]]) for j in range(16)]
            ev_e2tm = bar("scalar", ev_e2tm_l)
            ev_gth = list(ev_g) + ev_e2tm_l

        if upto == "ex":
            finish()
            return nc

        with ExitStack() as _sctx:
            qst0 = _sctx.enter_context(nc.sbuf_tensor("qst0", [128, 512], F32))
            qst1 = _sctx.enter_context(nc.sbuf_tensor("qst1", [128, 512], F32))
            qbf = _sctx.enter_context(nc.sbuf_tensor("qbf", [128, 512], BF16))
            qsc = _sctx.enter_context(nc.sbuf_tensor("qsc", [128, 512], F32))
            qst = [qst0, qst1]

            p.stage_tag = "s3"
            with ExitStack() as _sctx:
                wq3 = _sctx.enter_context(nc.sbuf_tensor("wq3s", [128, DC * 1024], BF16))
                wk3 = _sctx.enter_context(nc.sbuf_tensor("wk3s", [128, DC * 1024], BF16))
                wv3 = _sctx.enter_context(nc.sbuf_tensor("wv3s", [128, DC * 1024], BF16))
                pw3 = _sctx.enter_context(nc.sbuf_tensor("pw3s", [128, DC * 1024], BF16))
                esl0b = _sctx.enter_context(nc.sbuf_tensor("esl0b", [128, 4096], BF16))
                esl1b = _sctx.enter_context(nc.sbuf_tensor("esl1b", [128, 4096], BF16))
                qs0b = _sctx.enter_context(nc.sbuf_tensor("qsl0b", [128, 8 * 512], BF16))
                qs1b = _sctx.enter_context(nc.sbuf_tensor("qsl1b", [128, 8 * 512], BF16))
                ks0b = _sctx.enter_context(nc.sbuf_tensor("ksl0b", [128, 8 * 512], BF16))
                ks1b = _sctx.enter_context(nc.sbuf_tensor("ksl1b", [128, 8 * 512], BF16))
                vs0b = _sctx.enter_context(nc.sbuf_tensor("vslb", [128, 4 * 1040], BF16))
                oslab3 = _sctx.enter_context(nc.sbuf_tensor("osl3", [128, 4096], BF16))
                ot3 = _sctx.enter_context(nc.sbuf_tensor("ot3", [128, 4096], BF16))
                ymha3 = _sctx.enter_context(nc.sbuf_tensor("ym3", [128, 4096], F32))
                xmbf3 = _sctx.enter_context(nc.sbuf_tensor("xmb3", [128, 4096], BF16))
                abuf3 = _sctx.enter_context(nc.sbuf_tensor("ab3", [128, 2048], BF16))
                evl = []
                for slab, nm in [(wq3, "wq_sa"), (wk3, "wk_sa"), (wv3, "wv_sa"), (pw3, "pw_sa")]:
                    for c in range(DC):
                        evl.append(dma("sync", slab[:, 1024 * c:1024 * (c + 1)],
                                       dsc[nm + "_bf"][128 * c:128 * (c + 1), :],
                                       waits=[ev_pc[nm], ev_e2tm] + fence_s2))
                ev_w3 = bar("sync", evl)
                p.stage_tag = "qB"
                ev_gth_b = bar("gpsimd", ev_gth + [ev_e2tm])
                ev_quant, _ = _quant_passB(p, din, dsc, qst, qbf, qsc, qcol, invg,
                                           qpart, qrow, nrm, ones_f, ones_fr, dma, mm,
                                           vop, act, bar, ps_nm, ev_qA, [ev_gth_b])
                p.stage_tag = "s3"

                ph3 = dict(phase=2, wq=wq3, wk=wk3, wv=wv3, wp=pw3, ev_w=ev_w3,
                           qs=[qs0b, qs1b], ks=[ks0b, ks1b], vs=vs0b, esl=[esl0b, esl1b],
                           oslab=oslab3, ot=ot3, ymha=ymha3, xmbf=xmbf3, xbf=None,
                           abuf=abuf3, msk=None, ev_msk=None, fence=[ev_e2tm],
                           pb="pb_sa", g="g_sa", b="b_sa", spill=dsc["xm2"], ev_e2tm=ev_e2tm)
                ev_sp3 = mha_stage(ph3)
                fence_s3 = [p.last_ev[e] for e in ("tensor", "vector", "scalar")
                            if p.last_ev[e]] + [dma_drain("sync")]

            if upto == "s3":
                ev_quant = None

            if upto == "s3":
                finish()
                return nc

            p.stage_tag = "s4"
            with ExitStack() as _sctx:
                w1s4 = _sctx.enter_context(nc.sbuf_tensor("w1s4", [128, DC * 4096], BF16))
                w2s4 = _sctx.enter_context(nc.sbuf_tensor("w2s4", [128, FC * 1024], BF16))
                xin4 = _sctx.enter_context(nc.sbuf_tensor("xin4", [128, 4096], BF16))
                h4 = _sctx.enter_context(nc.sbuf_tensor("h4", [128, 16 * 512], BF16))
                usl4 = _sctx.enter_context(nc.sbuf_tensor("u4sl", [128, 4096], F32))
                evl = [dma("sync", w1s4[:, 4096 * c:4096 * (c + 1)],
                           dsc["w1_pf_bf"][128 * c:128 * (c + 1), :],
                           waits=[ev_pc["w1_pf"]] + fence_s3) for c in range(DC)]
                ev_w14 = bar("sync", evl)
                evl = [dma("sync", w2s4[:, 1024 * fc:1024 * (fc + 1)],
                           dsc["w2_pf_bf"][128 * fc:128 * (fc + 1), :],
                           waits=[ev_pc["w2_pf"]]) for fc in range(FC)]
                ev_w24 = bar("sync", evl)

                u2_evs = [[None] * DC for _ in range(NSL)]
                def post_pf(s, evs):
                    for c in range(DC):
                        u2_evs[s][c] = dma("sync", dsc["u2"][128 * c:128 * (c + 1), 512 * s:512 * (s + 1)],
                                           usl4[:, 512 * c:512 * (c + 1)], waits=[evs[c]])
                    return u2_evs[s]

                ph4 = dict(w1=w1s4, w2=w2s4, ev_w1=ev_w14, ev_w2=ev_w24, xin=xin4, h=h4,
                           u=usl4, ubf=usl4, b1="b1_pf", b2="b2_pf", g="g_pf", b="b_pf",
                           spill=dsc["xm2"], sp_evs=ev_sp3, post=post_pf)
                ffn_stage(ph4)
                fence_s4 = [p.last_ev[e] for e in ("tensor", "vector", "scalar")
                            if p.last_ev[e]] + [dma_drain("sync"), ev_quant]

        if upto == "s4":
            finish()
            return nc

        p.stage_tag = "s5"
        with ExitStack() as _sctx:
            wq1s = _sctx.enter_context(nc.sbuf_tensor("wq1s5", [128, DC * 4096], BF16))
            wq2s = _sctx.enter_context(nc.sbuf_tensor("wq2s5", [128, FC * 1024], BF16))
            x1s = _sctx.enter_context(nc.sbuf_tensor("x1s", [128, 4096], F32))
            z3s = _sctx.enter_context(nc.sbuf_tensor("z3s", [128, 4096], BF16))
            h5 = _sctx.enter_context(nc.sbuf_tensor("h5", [128, 16 * 512], BF16))
            xs5a = _sctx.enter_context(nc.sbuf_tensor("xs5a", [128, 512], F32))
            xs5b = _sctx.enter_context(nc.sbuf_tensor("xs5b", [128, 512], F32))
            xs5 = [xs5a, xs5b]
            evl = [dma("sync", wq1s[:, 4096 * c:4096 * (c + 1)],
                       dsc["wq1"][128 * c:128 * (c + 1), :],
                       waits=[ev_quant] + fence_s4) for c in range(DC)]
            ev_wq1 = bar("sync", evl)
            evl = [dma("sync", wq2s[:, 1024 * fc:1024 * (fc + 1)],
                       dsc["wq2"][128 * fc:128 * (fc + 1), :],
                       waits=[ev_quant]) for fc in range(FC)]
            ev_wq2 = bar("sync", evl)

            ev_out = []
            war5 = dict(x1=[None] * DC, z3=None, h=[None, None], xs=[None, None],
                        fsc=[None, None])
            for s in range(NSL):
                ev_x1 = []
                for c in range(DC):
                    evu = dma("sync", x1s[:, 512 * c:512 * (c + 1)],
                              dsc["u2"][128 * c:128 * (c + 1), 512 * s:512 * (s + 1)],
                              waits=[u2_evs[s][c]] + ([war5["x1"][c]] if war5["x1"][c] else []))
                    evx = dma("sync", xs5[c % 2][:, :],
                              din["x_str_t"][128 * c:128 * (c + 1), 512 * s:512 * (s + 1)],
                              waits=[war5["xs"][c % 2]] if war5["xs"][c % 2] else [])
                    ev_ux = bar("sync", [evu, evx])
                    ev_x1.append(vop(lambda e, c=c: e.tensor_tensor(
                        x1s[:, 512 * c:512 * (c + 1)], x1s[:, 512 * c:512 * (c + 1)],
                        xs5[c % 2][:, :], ALU.add), waits=[ev_ux]))
                    war5["xs"][c % 2] = ev_x1[-1]

                def ln5_out(c, ev_rb, ev_mrb):
                    ev1 = vop(lambda e, c=c: e.tensor_tensor(
                        tsc[c % 2][:, :], x1s[:, 512 * c:512 * (c + 1)], rb[:, :], ALU.mult),
                        waits=[ev_rb, ev_x1[c]])
                    w = [ev1, ev_mrb] + ([war5["z3"]] if war5["z3"] and c == 0 else [])
                    vop(lambda e, c=c: e.tensor_tensor(
                        z3s[:, 512 * c:512 * (c + 1)], tsc[c % 2][:, :], mrb[:, :],
                        ALU.subtract), waits=w, mark=False)
                    return vsync()
                evz = layernorm(ev_x1, lambda c: x1s[:, 512 * c:512 * (c + 1)], ln5_out)
                war5["z3"] = None

                ev_evac = [None] * DC
                ev_last_ad = None
                for halfI in range(2):
                    ev_h = [None] * 16
                    ev_bigfree = [None, None]
                    for f16 in range(16):
                        fc = 16 * halfI + f16
                        bank = ps_big[f16 % 2]
                        ev_m = None
                        for c in range(DC):
                            w = [ev_wq1, evz[c]]
                            if c == 0 and ev_bigfree[f16 % 2] is not None:
                                w.append(ev_bigfree[f16 % 2])
                            if c == 0 and f16 == 0 and war5["h"][halfI] is not None:
                                w.append(war5["h"][halfI])
                            ev_m = mm(bank[:, :],
                                      wq1s[:, 4096 * c + 128 * fc:4096 * c + 128 * (fc + 1)],
                                      z3s[:, 512 * c:512 * (c + 1)], c == 0, c == DC - 1,
                                      waits=w, mark=(c == DC - 1))
                        ev_h[f16] = act(h5[:, 512 * f16:512 * (f16 + 1)], bank[:, :], AF.Relu,
                                        bias=prm[:, prm_slots["b1_q"] + fc:prm_slots["b1_q"] + fc + 1],
                                        waits=[ev_m, ev_prm])
                        ev_bigfree[f16 % 2] = ev_h[f16]
                    if halfI == 0:
                        ev_scfree = [None, None]
                    else:
                        ev_scfree = [ev_evac[DC - 2], ev_evac[DC - 1]]
                    ev_wm_last = None
                    for c in range(DC):
                        bank = ps_sc[c % 2]
                        ev_m = None
                        for f16 in range(16):
                            fc = 16 * halfI + f16
                            w = [ev_wq2, ev_h[f16]]
                            if f16 == 0 and ev_scfree[c % 2] is not None:
                                w.append(ev_scfree[c % 2])
                            ev_m = mm(bank[:, :],
                                      wq2s[:, 1024 * fc + 128 * c:1024 * fc + 128 * (c + 1)],
                                      h5[:, 512 * f16:512 * (f16 + 1)], f16 == 0, f16 == 15,
                                      waits=w, mark=(f16 == 15))
                        ev_wm_last = ev_m
                        evf = act(fsc[c % 2][:, :], bank[:, :],
                                  AF.Identity if halfI == 0 else AF.Copy,
                                  bias=(prm[:, prm_slots["b2_q"] + c:prm_slots["b2_q"] + c + 1]
                                        if halfI == 0 else 0.0),
                                  waits=[ev_m, ev_prm] +
                                        ([war5["fsc"][c % 2]] if war5["fsc"][c % 2] else []))
                        ev_scfree[c % 2] = evf
                        wadd = [evf, evz[c]] + war_st[0] + st_sqb[0]
                        if halfI == 1:
                            wadd.append(ev_evac[c])
                        ev_ad = vop(lambda e, c=c: e.tensor_tensor(
                            x1s[:, 512 * c:512 * (c + 1)], x1s[:, 512 * c:512 * (c + 1)],
                            fsc[c % 2][:, :], ALU.add), waits=wadd)
                        war5["fsc"][c % 2] = ev_ad
                        if halfI == 0:
                            ev_evac[c] = ev_ad
                        else:
                            ev_last_ad = ev_ad
                            evo = dma("sync", out_d[128 * c:128 * (c + 1), 512 * s:512 * (s + 1)],
                                      x1s[:, 512 * c:512 * (c + 1)], waits=[ev_ad])
                            war5["x1"][c] = evo
                            ev_out.append(evo)
                    war5["h"] = [ev_wm_last, ev_wm_last]
                war5["z3"] = ev_h[15]
            for eo in ev_out:
                p.waits("sync", [eo])

        for _e in ("sync", "scalar", "gpsimd"):
            for _i in range(P.NCH):
                _s = f"d{_e}{_i}"
                if p.cnt[_s] > 0:
                    p.wait(_e, (_s, p.cnt[_s]))
        finish()
    return nc


def _quant_passA(p, din, qst, qcol, dma, act, vop, bar, fence):
    war = [None, None]
    ci = 0
    for nm, nrows, nsub in (("w1t_q", 8, 8), ("w2t_q", 32, 2)):
        for r in range(nrows):
            for sB in range(nsub):
                par = ci % 2
                stg = qst[par][:, :]
                evd = dma("gpsimd", stg,
                          din[nm][128 * r:128 * (r + 1), 512 * sB:512 * (sB + 1)],
                          waits=([war[par]] if war[par] else fence))
                evd = bar("gpsimd", [evd])
                eva = act(stg, stg, AF.Abs, waits=[evd])
                evr = vop(lambda e, stg=stg, ci=ci: e.reduce_sum(
                    qcol[:, ci:ci + 1], stg, axis=AX.X), waits=[eva])
                war[par] = evr
                ci += 1
    return war


def _quant_passB(p, din, dsc, qst, qbf, qsc, qcol, invg, qpart, qrow, nrm, ones_f,
                 ones_fr, dma, mm, vop, act, bar, ps_nm, ev_qA, fence):
    ev_r = []
    for mi, (c0, n) in enumerate([(0, 64), (64, 64)]):
        ev_p = vop(lambda e, mi=mi, c0=c0, n=n: e.reduce_sum(
            qpart[:, mi:mi + 1], qcol[:, c0:c0 + n], axis=AX.X),
            waits=[ev_qA[0], ev_qA[1]])
        d1 = dma("gpsimd", dsc["bc_q"][mi:mi + 1, :], qpart[:, mi:mi + 1], waits=[ev_p])
        d2 = dma("gpsimd", qrow[0:1, :], dsc["bc_q"][mi:mi + 1, :],
                 waits=[d1] + ([ev_r[0]] if mi else []))
        d2 = bar("gpsimd", [d2])
        evs = vop(lambda e, mi=mi: e.reduce_sum(
            nrm[0:1, 40 + mi:41 + mi], qrow[0:1, :], axis=AX.X),
            waits=[d2])
        ev1 = vop(lambda e, mi=mi: e.tensor_scalar(
            nrm[0:1, 44 + mi:45 + mi], nrm[0:1, 40 + mi:41 + mi],
            1.0 / 4194304.0, 1e-5, ALU.mult, ALU.add), waits=[evs])
        ev_r.append(vop(lambda e, mi=mi: e.tensor_scalar(
            nrm[0:1, 46 + mi:47 + mi], nrm[0:1, 44 + mi:45 + mi],
            0.5, None, ALU.mult), waits=[ev1]))
        ev_r.append(vop(lambda e, mi=mi: e.tensor_scalar(
            nrm[0:1, 48 + mi:49 + mi], nrm[0:1, 44 + mi:45 + mi],
            -0.5, None, ALU.mult), waits=[ev1]))
    ev_w = dma("gpsimd", dsc["bc_iv"], nrm[0:1, 46:50], waits=ev_r)
    ev_gb = dma("gpsimd", invg[:, 0:4], dsc["bc_iv"].broadcast_to([128, 4]), waits=[ev_w])
    ev_gb = dma("gpsimd", invg[:, 0:4], dsc["bc_iv"].broadcast_to([128, 4]), waits=[ev_gb])
    p.waits("gpsimd", [ev_gb])
    ev_gb = p.mark_now("gpsimd")
    ev_g = [ev_gb, ev_gb]
    war = [None, None]
    war_qsc = [None]
    last = None
    ci = 0
    for pi, (nm, nrows, nsub) in enumerate((("w1t_q", 8, 8), ("w2t_q", 32, 2))):
        dstd = dsc["wq1"] if nm == "w1t_q" else dsc["wq2"]
        for r in range(nrows):
            for sB in range(nsub):
                par = ci % 2
                stg = qst[par][:, :]
                evd = dma("gpsimd", stg,
                          din[nm][128 * r:128 * (r + 1), 512 * sB:512 * (sB + 1)],
                          waits=[war[par]] if war[par] else [ev_g[pi]])
                evd = bar("gpsimd", [evd])
                v1 = vop(lambda e, stg=stg, pi=pi: e.tensor_scalar(
                    qsc[:, :], stg, invg[:, pi:pi + 1], None, ALU.is_gt),
                    waits=[evd, ev_g[pi]] + ([war_qsc[0]] if war_qsc[0] else []))
                v2 = vop(lambda e, stg=stg, pi=pi: e.tensor_scalar(
                    stg, stg, invg[:, 2 + pi:3 + pi], None, ALU.is_lt), waits=[v1])
                v3 = vop(lambda e, stg=stg: e.tensor_tensor(
                    qbf[:, :], qsc[:, :], stg, ALU.subtract),
                    waits=[v2] + ([last] if last else []))
                war_qsc[0] = v3
                last = dma("gpsimd", dstd[128 * r:128 * (r + 1), 512 * sB:512 * (sB + 1)],
                           qbf[:, :], waits=[v3])
                war[par] = last
                ci += 1
    return last, ev_r[-1]


_ncache = {}
_LAST_RES = None

def kernel(**inputs):
    global _LAST_RES
    in_maps = _layout_inputs(inputs)
    if "nc" not in _ncache:
        _ncache["nc"] = build_nc()
    res = run_bass_kernel_spmd(_ncache["nc"], in_maps, core_ids=list(range(NCORES)))
    _LAST_RES = res.results
    return _assemble(res.results)


# version 04
# speedup vs baseline: 1.0531x; 1.0531x over previous
import numpy as np
import ml_dtypes

import concourse.bass as bass
import concourse.mybir as mybir
from concourse.bass_utils import run_bass_kernel_spmd

F32 = mybir.dt.float32
BF16 = mybir.dt.bfloat16
AF = mybir.ActivationFunctionType
ALU = mybir.AluOpType
AX = mybir.AxisListType
ds = bass.ds
bf16 = ml_dtypes.bfloat16

NCORES = 8
D = 1024
DC = 8
H = 16
DI = 4096
FC = 32
T = 2048
NSL = 4
EPS = 1e-5
RND = 12582912.0

_DEBUG = {"upto": None, "spill_out": False, "dbg_s": 0}


def _layout_inputs(inp):
    x0 = np.ascontiguousarray(np.asarray(inp["L0_attn_enc_input"], dtype=np.float32)[0])

    def packqkv(w):
        return np.ascontiguousarray(np.asarray(w).transpose(1, 0, 2).reshape(D, H * 64))

    def cols(v, n):
        return np.ascontiguousarray(np.asarray(v).reshape(n, 128).T)

    shared = {
        "wq_la": packqkv(inp["L0_attn_la_wqs"]), "wk_la": packqkv(inp["L0_attn_la_wks"]),
        "wv_la": packqkv(inp["L0_attn_la_wvs"]), "pw_la": np.ascontiguousarray(np.asarray(inp["L0_attn_la_pw"])),
        "wq_sa": packqkv(inp["L0_attn_sa_wqs"]), "wk_sa": packqkv(inp["L0_attn_sa_wks"]),
        "wv_sa": packqkv(inp["L0_attn_sa_wvs"]), "pw_sa": np.ascontiguousarray(np.asarray(inp["L0_attn_sa_pw"])),
        "w1_lf": np.ascontiguousarray(np.asarray(inp["L0_attn_lf_w1"])),
        "w2_lf": np.ascontiguousarray(np.asarray(inp["L0_attn_lf_w2"])),
        "w1_pf": np.ascontiguousarray(np.asarray(inp["L0_attn_pf_w1"])),
        "w2_pf": np.ascontiguousarray(np.asarray(inp["L0_attn_pf_w2"])),
        "w1t_q": np.ascontiguousarray(np.asarray(inp["L0_ffn_W1"]).T),
        "w2t_q": np.ascontiguousarray(np.asarray(inp["L0_ffn_W2"]).T),
        "pb_la": cols(inp["L0_attn_la_pb"], DC), "g_la": cols(inp["L0_attn_la_g"], DC),
        "b_la": cols(inp["L0_attn_la_b"], DC), "b1_lf": cols(inp["L0_attn_lf_b1"], FC),
        "b2_lf": cols(inp["L0_attn_lf_b2"], DC), "g_lf": cols(inp["L0_attn_lf_g"], DC),
        "b_lf": cols(inp["L0_attn_lf_b"], DC), "pb_sa": cols(inp["L0_attn_sa_pb"], DC),
        "g_sa": cols(inp["L0_attn_sa_g"], DC), "b_sa": cols(inp["L0_attn_sa_b"], DC),
        "b1_pf": cols(inp["L0_attn_pf_b1"], FC), "b2_pf": cols(inp["L0_attn_pf_b2"], DC),
        "g_pf": cols(inp["L0_attn_pf_g"], DC), "b_pf": cols(inp["L0_attn_pf_b"], DC),
        "b1_q": cols(inp["L0_ffn_b1"], FC), "b2_q": cols(inp["L0_ffn_b2"], DC),
    }
    x0r = x0.reshape(512, 32, D)
    in_maps = []
    for c in range(NCORES):
        m = dict(shared)
        m["x_win_t"] = np.ascontiguousarray(x0[2048 * c:2048 * (c + 1)].T)
        m["x_str_t"] = np.ascontiguousarray(
            x0r[:, 4 * c:4 * c + 4, :].transpose(2, 1, 0).reshape(D, T))
        in_maps.append(m)
    return in_maps


def _assemble(results):
    out = np.empty((16384, D), dtype=np.float32)
    outr = out.reshape(512, 32, D)
    for c in range(NCORES):
        y = np.asarray(results[c]["out"])
        yt = y.T.reshape(4, 512, D)
        for il in range(4):
            outr[:, 4 * c + il, :] = yt[il]
    return out.reshape(1, 16384, D)


class P:

    ENGS = ("sync", "tensor", "vector", "scalar", "gpsimd")
    NCH = 24

    def __init__(self, nc, ctx):
        self.nc = nc
        self.q = {e: [] for e in self.ENGS}
        self.sems = {}
        self.cnt = {}
        self.owner = {}
        self.lastw = {e: {} for e in self.ENGS}
        self.dcnt = {}
        for e in self.ENGS:
            name = "m" + e
            self.sems[name] = ctx.enter_context(nc.semaphore(name))
            self.cnt[name] = 0
            self.owner[name] = e
        for e in ("sync", "scalar", "gpsimd"):
            self.dcnt[e] = 0
            for i in range(self.NCH):
                name = f"d{e}{i}"
                self.sems[name] = ctx.enter_context(nc.semaphore(name))
                self.cnt[name] = 0
                self.owner[name] = e
        self.sems["cc"] = ctx.enter_context(nc.semaphore("cc"))
        self.cnt["cc"] = 0
        self.owner["cc"] = "gpsimd"
        self.last_ev = {e: None for e in self.ENGS}
        self.stage_tag = "init"

    def wait(self, eng, ev):
        if ev is None:
            return
        s, v = ev
        if self.lastw[eng].get(s, 0) >= v:
            return
        self.lastw[eng][s] = v
        self.q[eng].append(("w", s, v, self.stage_tag))

    def waits(self, eng, evs):
        for ev in evs:
            self.wait(eng, ev)

    def do(self, eng, fn, mark=False, dma=False, cc=False):
        ev = None
        sem = None
        inc = 1
        if cc:
            sem, inc = "cc", 1
            self.cnt[sem] += inc
            ev = (sem, self.cnt[sem])
        elif dma:
            i = self.dcnt[eng]
            self.dcnt[eng] = i + 1
            sem = f"d{eng}{i % self.NCH}"
            inc = 16
            if i >= self.NCH:
                self.wait(eng, (sem, self.cnt[sem]))
            self.cnt[sem] += 16
            ev = (sem, self.cnt[sem])
        elif mark:
            sem = "m" + eng
            self.cnt[sem] += 1
            ev = (sem, self.cnt[sem])
        self.q[eng].append(("i", fn, sem, inc, self.stage_tag))
        if ev is not None:
            self.last_ev[eng] = ev
        return ev

    def mark_now(self, eng):
        return self.do(eng, lambda e: e.nop(), mark=True)

    def replay(self, block):
        def mk(eng_name):
            def body(eng):
                cur = [None, None]
                def set_scope(tag):
                    if tag == cur[0]:
                        return
                    if cur[0] is not None:
                        self.nc.leave_named_scope(f"{cur[0]}_{eng_name}", cur[1], False)
                    sid, _ = self.nc.enter_named_scope(f"{tag}_{eng_name}", False)
                    cur[0], cur[1] = tag, sid
                for item in self.q[eng_name]:
                    set_scope(item[-1])
                    if item[0] == "w":
                        eng.wait_ge(self.sems[item[1]], item[2])
                    else:
                        _, fn, sem, inc, _tag = item
                        ins = fn(eng)
                        if sem is not None:
                            ins.then_inc(self.sems[sem], inc)
                if cur[0] is not None:
                    self.nc.leave_named_scope(f"{cur[0]}_{eng_name}", cur[1], False)
            return body
        block.sync(mk("sync"))
        block.tensor(mk("tensor"))
        block.vector(mk("vector"))
        block.scalar(mk("scalar"))
        block.gpsimd(mk("gpsimd"))


def build_nc():
    from contextlib import ExitStack
    nc = bass.Bass("TRN2", target_bir_lowering=False,
                   detect_race_conditions=not _DEBUG.get("no_race", False))
    upto = _DEBUG.get("upto")
    spill_kind = "ExternalOutput" if _DEBUG.get("spill_out") else "Internal"

    din = {}
    for name, shape in [
        ("x_win_t", [D, T]), ("x_str_t", [D, T]),
        ("wq_la", [D, D]), ("wk_la", [D, D]), ("wv_la", [D, D]), ("pw_la", [D, D]),
        ("wq_sa", [D, D]), ("wk_sa", [D, D]), ("wv_sa", [D, D]), ("pw_sa", [D, D]),
        ("w1_lf", [D, DI]), ("w2_lf", [DI, D]), ("w1_pf", [D, DI]), ("w2_pf", [DI, D]),
        ("w1t_q", [D, DI]), ("w2t_q", [DI, D]),
        ("pb_la", [128, DC]), ("g_la", [128, DC]), ("b_la", [128, DC]),
        ("b1_lf", [128, FC]), ("b2_lf", [128, DC]), ("g_lf", [128, DC]), ("b_lf", [128, DC]),
        ("pb_sa", [128, DC]), ("g_sa", [128, DC]), ("b_sa", [128, DC]),
        ("b1_pf", [128, FC]), ("b2_pf", [128, DC]), ("g_pf", [128, DC]), ("b_pf", [128, DC]),
        ("b1_q", [128, FC]), ("b2_q", [128, DC]),
    ]:
        din[name] = nc.dram_tensor(name, shape, F32, kind="ExternalInput").ap()
    out_d = nc.dram_tensor("out", [D, T], F32, kind="ExternalOutput").ap()

    dsc = {
        "x_win_bf": nc.dram_tensor("x_win_bf", [D, T], BF16, kind="Internal").ap(),
        "xm1": nc.dram_tensor("xm1", [D, T], BF16, kind=spill_kind).ap(),
        "xm2": nc.dram_tensor("xm2", [D, T], BF16, kind=spill_kind).ap(),
        "u2": nc.dram_tensor("u2", [D, T], F32, kind=spill_kind).ap(),
        "cc_in": nc.dram_tensor("cc_in", [T, D], BF16,
                                kind=spill_kind if upto in ("s1", "s2") else "Internal").ap(),
        "cc_out": nc.dram_tensor("cc_out", [NCORES * T, D], BF16, kind="Internal",
                                 addr_space="Shared").ap(),
        "e2tm": nc.dram_tensor("e2tm", [T, D], BF16, kind=spill_kind).ap(),
        "wq1": nc.dram_tensor("wq1", [D, DI], BF16, kind=spill_kind).ap(),
        "wq2": nc.dram_tensor("wq2", [DI, D], BF16, kind="Internal").ap(),
        "bc_ln": nc.dram_tensor("bc_ln", [1, 1024], F32, kind="Internal").ap(),
        "bc_nc": nc.dram_tensor("bc_nc", [1, 8], F32, kind="Internal").ap(),
        "bc_iv": nc.dram_tensor("bc_iv", [1, 4], F32, kind="Internal").ap(),
        "bc_q": nc.dram_tensor("bc_q", [2, 128], F32, kind="Internal").ap(),
    }
    if _DEBUG.get("spill_out"):
        for nm, shp, dt in [("dbg_e", [D, T], BF16), ("dbg_q", [128, 4096], BF16),
                            ("dbg_k", [128, 4096], BF16), ("dbg_o", [128, 4096], BF16),
                            ("dbg_negc", [128, 8], F32), ("dbg_y", [128, 4096], F32),
                            ("dbg_a", [128, 512], BF16), ("dbg_x", [128, 4096], BF16),
                            ("dbg_w", [128, 8192], BF16), ("dbg_v", [128, 4160], BF16),
                            ("dbg_rows", [1, 2048], F32), ("dbg_rb", [128, 1024], F32),
                            ("dbg_xm", [128, 4096], BF16), ("dbg_mrb", [128, 1024], F32),
                            ("dbg_mrows", [1, 2048], F32), ("dbg_f_x", [128, 4096], BF16),
                            ("dbg_f_h", [128, 8192], BF16), ("dbg_f_u", [128, 4096], F32)]:
            dsc[nm] = nc.dram_tensor(nm, shp, dt, kind="ExternalOutput").ap()
    for nm, shp in [("wq_sa", [D, D]), ("wk_sa", [D, D]), ("wv_sa", [D, D]), ("pw_sa", [D, D]),
                    ("w1_lf", [D, DI]), ("w2_lf", [DI, D]), ("w1_pf", [D, DI]), ("w2_pf", [DI, D])]:
        dsc[nm + "_bf"] = nc.dram_tensor(nm + "_bf", shp, BF16, kind="Internal").ap()

    ident = nc.inline_tensor(np.eye(128, dtype=bf16), "ident").ap()
    maskblk = np.kron(np.eye(4, dtype=np.float32), np.ones((32, 32), np.float32))
    mask512 = nc.inline_tensor(np.tile(maskblk, (1, 4)).astype(bf16), "mask512").ap()
    ones_bf_c = nc.inline_tensor(np.ones((128, 1), dtype=bf16), "ones_bf").ap()
    ones128_c = nc.inline_tensor(np.ones((128, 128), dtype=bf16), "ones128").ap()
    ones_f_c = nc.inline_tensor(np.ones((128, 1), dtype=np.float32), "ones_f").ap()
    ones_f_r = nc.inline_tensor(np.ones((1, 128), dtype=np.float32), "ones_fr").ap()

    ctx = ExitStack()
    with ctx:
        p = P(nc, ctx)
        sb = lambda name, shape, dt: ctx.enter_context(nc.sbuf_tensor(name, shape, dt))

        prm = sb("prm", [128, 13 * DC + 3 * FC], F32)
        prm_slots = {}
        off = 0
        for nm in ("pb_la", "g_la", "b_la", "b2_lf", "g_lf", "b_lf", "pb_sa", "g_sa",
                   "b_sa", "b2_pf", "g_pf", "b_pf", "b2_q"):
            prm_slots[nm] = off
            off += DC
        for nm in ("b1_lf", "b1_pf", "b1_q"):
            prm_slots[nm] = off
            off += FC
        id_sb = sb("id_sb", [128, 128], BF16)
        ones128 = sb("ones128s", [128, 128], BF16)
        ones_bf = sb("ones_bfs", [128, 1], BF16)
        ones_f = sb("ones_fs", [128, 1], F32)
        ones_fr = sb("ones_frs", [1, 128], F32)
        rows = sb("rows", [1, 2048], F32)
        nrm = sb("nrm", [1, 64], F32)
        vscr = sb("vscr", [1, 128], F32)
        negc = sb("negc", [128, 8], F32)
        linv = sb("linv", [128, 4], F32)
        invg = sb("invg", [128, 4], F32)
        qpart = sb("qpart", [128, 2], F32)
        qrow = sb("qrow", [1, 128], F32)
        qcol = sb("qcol", [128, 128], F32)
        rb = sb("rb", [128, 512], F32)
        mrb = sb("mrb", [128, 512], F32)
        tsc = [sb(f"tsc{i}", [128, 512], F32) for i in range(2)]
        fsc = [sb(f"fsc{i}", [128, 512], F32) for i in range(2)]
        bsc = [None, None] + [sb(f"bsc{i}", [128, 512], BF16) for i in (2, 3)]
        sqb = sb("sqb", [128, 4 * 512], BF16)

        ps_big = [ctx.enter_context(nc.psum_tensor(f"big{i}", [128, 512], F32)) for i in range(2)]
        ps_sc = [ctx.enter_context(nc.psum_tensor(f"sc{i}", [128, 512], F32)) for i in range(2)]
        ps_o = ctx.enter_context(nc.psum_tensor("ps_o", [128, 512], F32))
        ps_tr = ctx.enter_context(nc.psum_tensor("ps_tr", [128, 512], F32))
        ps_st = ctx.enter_context(nc.psum_tensor("ps_st", [128, 512], F32))
        ps_nm = ctx.enter_context(nc.psum_tensor("ps_nm", [128, 512], F32))

        def dma(eng, out, in_, waits=()):
            p.waits(eng, waits)
            return p.do(eng, lambda e, o=out, i=in_: e.dma_start(out=o, in_=i), mark=True, dma=True)

        def mm(out, lhsT, rhs, start, stop, waits=(), mark=False):
            p.waits("tensor", waits)
            return p.do("tensor", lambda e, o=out, l=lhsT, r=rhs, s1=start, s2=stop:
                        e.matmul(o, l, r, start=s1, stop=s2), mark=mark)

        def act(out, in_, func, bias=0.0, scale=1.0, waits=(), mark=True):
            p.waits("scalar", waits)
            return p.do("scalar", lambda e, o=out, i=in_, f=func, b=bias, s=scale:
                        e.activation(o, i, f, bias=b, scale=s), mark=mark)

        def vop(fn, waits=(), mark=True):
            p.waits("vector", waits)
            return p.do("vector", fn, mark=mark)

        def bar(eng, evs):
            p.waits(eng, evs)
            return p.mark_now(eng)

        _vs_last = [None]

        def vsync():
            ev = vop(lambda e: e.tensor_scalar(vscr[0:1, :], vscr[0:1, :],
                                               1.0, None, ALU.mult),
                     waits=[_vs_last[0]] if _vs_last[0] else [])
            _vs_last[0] = ev
            return ev

        def dma_drain(eng):
            evs = [(f"d{eng}{i}", p.cnt[f"d{eng}{i}"]) for i in range(P.NCH)
                   if p.cnt[f"d{eng}{i}"] > 0]
            return bar(eng, evs)

        def finish():
            for _e in ("sync", "scalar", "gpsimd"):
                for _i in range(P.NCH):
                    _s = f"d{_e}{_i}"
                    if p.cnt[_s] > 0:
                        p.wait(_e, (_s, p.cnt[_s]))
            with nc.Block() as block:
                p.replay(block)

        pidc = [None]
        def _getpid(e):
            pidc[0] = e.partition_id()
            return e.nop()
        p.do("scalar", _getpid)
        ev_init = []
        for nm in prm_slots:
            n = FC if nm.startswith("b1") else DC
            o = prm_slots[nm]
            ev_init.append(dma("sync", prm[:, o:o + n], din[nm]))
        for dst, src in [(id_sb, ident), (ones_bf, ones_bf_c), (ones_f, ones_f_c),
                         (ones_fr, ones_f_r), (ones128, ones128_c)]:
            ev_init.append(dma("sync", dst[:, :], src))
        ev_prm = bar("sync", ev_init)
        ev_negc0 = vop(lambda e: e.memset(negc[:, 0:8], -30.0))
        _vs_last[0] = vop(lambda e: e.memset(vscr[0:1, :], 0.0))

        ev_xw = [dma("gpsimd", dsc["x_win_bf"][:, 512 * s:512 * (s + 1)],
                     din["x_win_t"][:, 512 * s:512 * (s + 1)]) for s in range(NSL)]

        st_sq2 = [None, None]
        st_sqb = [[] ]
        war_rb = []
        war_st = [None]
        war_rows = [None, None]
        war_bc = [None, None]
        war_bcn = [None]

        def layernorm(src_evs, src, out_fn):
            ev_m1e = None
            ev_mmA = None
            ev_mmB = None
            last_sq = None
            wr0 = [w for w in war_rows if w]
            evA = None
            wst = war_st[0] or []
            for c in range(DC):
                xap = src(c)
                isbf = xap.tensor.dtype == BF16
                ev_m1e = mm(ps_st[0:1, :], (ones_bf if isbf else ones_f)[:, :], xap,
                            c == 0, c == DC - 1,
                            waits=[src_evs[c]] + (wst if c == 0 else []),
                            mark=(c == DC - 1))
            for c in range(DC):
                xap = src(c)
                slot = c % 4
                w = [src_evs[c]] + (st_sqb[0] if c == 0 else [])
                if c >= 4:
                    w.append(ev_mmA)
                vop(lambda e, xap=xap, slot=slot: e.tensor_tensor(
                    sqb[:, 512 * slot:512 * (slot + 1)], xap, xap, ALU.mult),
                    waits=w, mark=False)
                ev_s = vsync()
                last_sq = ev_s
                w2 = [ev_s] + (wst if c == 0 else [])
                if c == 4:
                    w2.append(evA)
                ev_m2 = mm(ps_nm[0:1, :], ones_bf[:, :], sqb[:, 512 * slot:512 * (slot + 1)],
                           c % 4 == 0, c % 4 == 3, waits=w2, mark=(c % 4 == 3))
                if c == 3:
                    ev_mmA = ev_m2
                    evA = act(rows[0:1, 512:1024], ps_nm[0:1, :], AF.Copy,
                              bias=0.0, scale=1.0 / D, waits=[ev_mmA] + wr0)
                elif c == 7:
                    ev_mmB = ev_m2
            st_sqb[0] = [ev_mmB, last_sq]
            wr = [w for w in war_rows if w]
            ev_ma = vop(lambda e: e.tensor_scalar(rows[0:1, 0:512], ps_st[0:1, :],
                                                  1.0 / D, None, ALU.mult),
                        waits=[ev_m1e] + wr)
            ev_r1 = vop(lambda e: e.scalar_tensor_tensor(
                rows[0:1, 512:1024], ps_nm[0:1, :], 1.0 / D, rows[0:1, 512:1024],
                ALU.mult, ALU.add), waits=[ev_mmB, evA])
            war_st[0] = [ev_ma, ev_r1]
            ev_c = vop(lambda e: e.tensor_tensor(rows[0:1, 1024:1536], rows[0:1, 0:512],
                                                 rows[0:1, 0:512], ALU.mult), waits=[ev_ma])
            ev_d = vop(lambda e: e.tensor_tensor(rows[0:1, 1024:1536], rows[0:1, 512:1024],
                                                 rows[0:1, 1024:1536], ALU.subtract),
                       waits=[ev_r1, ev_c])
            ev_e_ = vop(lambda e: e.tensor_scalar(rows[0:1, 1024:1536], rows[0:1, 1024:1536],
                                                  EPS, None, ALU.add), waits=[ev_d])
            ev_var = vop(lambda e: e.tensor_scalar(rows[0:1, 1024:1536], rows[0:1, 1024:1536],
                                                   1e-30, None, ALU.max), waits=[ev_e_])
            ev_ln = act(rows[0:1, 512:1024], rows[0:1, 1024:1536], AF.Ln, waits=[ev_var])
            ev_rst = act(rows[0:1, 1536:2048], rows[0:1, 512:1024], AF.Exp,
                         bias=0.0, scale=-0.5, waits=[ev_ln])
            ev_mr = vop(lambda e: e.scalar_tensor_tensor(rows[0:1, 1024:1536], rows[0:1, 0:512],
                                                         1.0, rows[0:1, 1536:2048],
                                                         ALU.mult, ALU.mult),
                        waits=[ev_rst, ev_ma, ev_var])
            wb = [w for w in war_rb if w]
            ev_w1 = dma("sync", dsc["bc_ln"][0:1, 0:512], rows[0:1, 1536:2048],
                        waits=[ev_rst] + ([war_bc[0]] if war_bc[0] else []))
            ev_w2 = dma("sync", dsc["bc_ln"][0:1, 512:1024], rows[0:1, 1024:1536],
                        waits=[ev_mr] + ([war_bc[1]] if war_bc[1] else []))
            ev_rb = dma("sync", rb[:, :], dsc["bc_ln"][0:1, 0:512].broadcast_to([128, 512]),
                        waits=[ev_w1] + wb)
            ev_mrb = dma("sync", mrb[:, :], dsc["bc_ln"][0:1, 512:1024].broadcast_to([128, 512]),
                         waits=[ev_w2] + wb)
            war_rows[:] = [ev_w1, ev_w2]
            war_bc[0], war_bc[1] = ev_rb, ev_mrb
            ev_bc = bar("sync", [ev_rb, ev_mrb])
            evs = [out_fn(c, ev_bc, ev_bc) for c in range(DC)]
            war_rb[:] = [evs[-1]]
            return evs

        def norm_shift(qslab, kslab, ev_q, ev_k, war_negc):
            ev_red = [None] * 16
            for idx in range(16):
                which, t = ("q", idx) if idx < 8 else ("k", idx - 8)
                slab = qslab if which == "q" else kslab
                evs = ev_q if which == "q" else ev_k
                par = idx % 2
                bank = ps_st if par == 0 else ps_nm
                ev_s = act(bsc[2 + par][:, :], slab[:, 512 * t:512 * (t + 1)], AF.Square,
                           waits=[evs[t]] + (st_sq2[par] or []))
                w = [ev_s]
                if idx < 2 and war_st[0]:
                    w.extend(war_st[0])
                if idx >= 2:
                    w.append(ev_red[idx - 2])
                ev_m = mm(bank[:, :], ones128[:, :], bsc[2 + par][:, :], True, True,
                          waits=w, mark=True)
                st_sq2[par] = [ev_m, ev_s]
                ev_red[idx] = vop(lambda e, bank=bank, idx=idx: e.reduce_max(
                    nrm[0:1, idx:idx + 1], bank[0:1, :], axis=AX.X), waits=[ev_m])
            ev1 = ev_red[14]
            ev2 = ev_red[15]
            war_st[0] = [ev2]
            ev3 = vop(lambda e: e.tensor_tensor(nrm[0:1, 16:24], nrm[0:1, 0:8],
                                                nrm[0:1, 8:16], ALU.mult), waits=[ev1, ev2])
            ev4 = act(nrm[0:1, 24:32], nrm[0:1, 16:24], AF.Sqrt, waits=[ev3])
            ev5 = vop(lambda e: e.tensor_scalar(nrm[0:1, 32:40], nrm[0:1, 24:32],
                                                -0.125, None, ALU.mult), waits=[ev4])
            ev_w = dma("sync", dsc["bc_nc"], nrm[0:1, 32:40],
                       waits=[ev5] + ([war_bcn[0]] if war_bcn[0] else []))
            ev6 = dma("sync", negc[:, 0:8], dsc["bc_nc"].broadcast_to([128, 8]),
                      waits=[ev_w] + ([war_negc] if war_negc else []))
            ev6 = dma("sync", negc[:, 0:8], dsc["bc_nc"].broadcast_to([128, 8]),
                      waits=[ev6])
            war_bcn[0] = ev6
            war_st[0] = [ev2]
            return bar("sync", [ev6])

        def mha_stage(ph):
            qslab, kslab, vslab = ph["qs"], ph["ks"], ph["vs"]
            esl_b = ph["esl"]
            war = dict(qk=[None, None], v=None, ab=[None] * 4, oslab=None,
                       ot=None, ymha=None, xbf=None, e=[None, None], negc=ev_negc0,
                       edump=[None, None], dbg=[], xmbf_dma=[])
            sp_evs = [[None] * DC for _ in range(NSL)]
            ev_vones = vop(lambda e: e.memset(
                vslab.rearrange("p (s h o) -> p s h o", s=4, h=H)[:, :, :, 64:65], 1.0),
                waits=ph.get("fence", []))

            for s in range(NSL):
                par = s % 2
                esl_t = esl_b[par]

                def esl(c, t=esl_t):
                    return t[:, 512 * c:512 * (c + 1)]

                if ph["phase"] == 1:
                    ev_x = []
                    for c in range(DC):
                        ev_x.append(dma("sync", ph["xbf"][:, 512 * c:512 * (c + 1)],
                                        dsc["x_win_bf"][128 * c:128 * (c + 1), 512 * s:512 * (s + 1)],
                                        waits=[ev_xw[s]] + ([war["xbf"]] if war["xbf"] and c == 0 else [])))
                    ev_x = [bar("sync", [e]) for e in ev_x]

                    def ln0_out(c, ev_rb, ev_mrb):
                        ev_p1 = vop(lambda e, c=c: e.tensor_tensor(
                            tsc[c % 2][:, :], ph["xbf"][:, 512 * c:512 * (c + 1)], rb[:, :],
                            ALU.mult), waits=[ev_rb] + list(war["dbg"]), mark=True)
                        if "dbg_rb" in dsc and s == _DEBUG.get("dbg_s", 0) and c == 0 and ph["phase"] == 1:
                            war["dbg"].append(dma("sync", dsc["dbg_y"][:, 0:512], tsc[0][:, :],
                                                  waits=[ev_p1]))
                        w = [ev_p1, ev_mrb] + ([war["e"][par]] if war["e"][par] and c == 0 else [])
                        if war["edump"][par] and c == 0:
                            w.append(war["edump"][par])
                        vop(lambda e, dst=esl(c), c=c: e.tensor_tensor(
                            dst, tsc[c % 2][:, :], mrb[:, :], ALU.subtract), waits=w,
                            mark=False)
                        return vsync()
                    if "dbg_x" in dsc and s == _DEBUG.get("dbg_s", 0) and ph["phase"] == 1:
                        dma("sync", dsc["dbg_x"], ph["xbf"][:, :], waits=ev_x)
                    ev_e = layernorm(ev_x, lambda c: ph["xbf"][:, 512 * c:512 * (c + 1)], ln0_out)
                    war["xbf"] = ev_e[-1]
                    if "dbg_rows" in dsc and s == _DEBUG.get("dbg_s", 0) and ph["phase"] == 1:
                        war_rows.append(dma("sync", dsc["dbg_rows"], rows[0:1, :], waits=[ev_e[0]]))
                        war_rb.append(dma("sync", dsc["dbg_rb"][:, 0:512], rb[:, :], waits=[ev_e[0]]))
                        war_rb.append(dma("sync", dsc["dbg_rb"][:, 512:1024], mrb[:, :], waits=[ev_e[0]]))
                    if "dbg_e" in dsc:
                        for c in range(DC):
                            war["edump"][par] = dma(
                                "sync", dsc["dbg_e"][128 * c:128 * (c + 1), 512 * s:512 * (s + 1)],
                                esl(c), waits=[ev_e[c]])
                else:
                    ev_e = []
                    for c in range(DC):
                        w = [ph["ev_e2tm"]] + ([war["e"][par]] if war["e"][par] and c == 0 else [])
                        p.waits("scalar", w)
                        ev_e.append(p.do("scalar", lambda e, dst=esl(c), c=c, s=s: e.dma_start_transpose(
                            dst, dsc["e2tm"][512 * s:512 * (s + 1), 128 * c:128 * (c + 1)]),
                            mark=True, dma=True))
                    ev_e = [bar("scalar", [e]) for e in ev_e]

                ev_q = [None] * 8
                ev_k = [None] * 8
                ev_bigfree = [None, None]
                ev_vlast = None
                bigpar = 0
                for kind, t in [("q", t) for t in range(8)] + [("k", t) for t in range(8)] + \
                               [("v", t) for t in range(8)]:
                    bank = ps_big[bigpar]
                    wslab = ph["w" + kind]
                    ev_g = None
                    for c in range(DC):
                        if kind == "v":
                            g, half = t // 2, t % 2
                            lhs = esl(c)[:, 128 * g:128 * (g + 1)]
                            rhs = wslab[:, 1024 * c + 512 * half:1024 * c + 512 * (half + 1)]
                        else:
                            lhs = wslab[:, 1024 * c + 128 * t:1024 * c + 128 * (t + 1)]
                            rhs = esl(c)
                        w = [ph["ev_w"], ev_e[c]]
                        if c == 0 and ev_bigfree[bigpar] is not None:
                            w.append(ev_bigfree[bigpar])
                        if war["dbg"]:
                            w.extend(war["dbg"])
                        ev_g = mm(bank[:, :], lhs, rhs, c == 0, c == DC - 1, waits=w,
                                  mark=(c == DC - 1))
                    if kind == "q":
                        ev_q[t] = act(qslab[par][:, 512 * t:512 * (t + 1)], bank[:, :], AF.Copy,
                                      waits=[ev_g] + ([war["qk"][par]] if war["qk"][par] else []))
                        ev_bigfree[bigpar] = ev_q[t]
                        if _DEBUG.get("trunc") == "q0" and t == 0 and ph["phase"] == 1:
                            dma("sync", dsc["dbg_q"][:, 0:512], qslab[par][:, 0:512], waits=[ev_q[0]])
                            return None
                    elif kind == "k":
                        ev_k[t] = act(kslab[par][:, 512 * t:512 * (t + 1)], bank[:, :], AF.Copy,
                                      waits=[ev_g])
                        ev_bigfree[bigpar] = ev_k[t]
                    else:
                        g, half = t // 2, t % 2
                        dst = vslab.rearrange("p (s h o) -> p s h o", s=4, h=H)[
                            :, g, 8 * half:8 * (half + 1), 0:64]
                        srcv = bank.rearrange("p (h o) -> p h o", h=8)
                        w = [ev_g, ev_vones] + ([war["v"]] if war["v"] else [])
                        ev_vlast = act(dst, srcv, AF.Copy, waits=w)
                        ev_bigfree[bigpar] = ev_vlast
                    bigpar ^= 1
                war["qk"][par] = None
                war["v"] = None

                ev_negc = norm_shift(qslab[par], kslab[par], ev_q, ev_k, war["negc"])
                if "dbg_q" in dsc and s == _DEBUG.get("dbg_s", 0) and ph["phase"] == 1:
                    war["dbg"].append(dma("sync", dsc["dbg_q"], qslab[par][:, :], waits=ev_q))
                    war["dbg"].append(dma("sync", dsc["dbg_k"], kslab[par][:, :], waits=ev_k))
                    war["dbg"].append(dma("sync", dsc["dbg_negc"], negc[:, 0:8], waits=[ev_negc]))
                if _DEBUG.get("trunc") == "qkv" and ph["phase"] == 1:
                    return None

                ev_heads = []
                for h in range(H):
                    pair, half = h // 2, h % 2
                    ob = ps_o if h % 2 == 0 else ps_tr
                    ev_exp = [None] * 4
                    if ph["phase"] == 1:
                        scb = ps_sc[h % 2]
                        ev_sc = None
                        for chk in range(4):
                            lhs = kslab[par][64 * half:64 * half + 64,
                                             512 * pair + 128 * chk:512 * pair + 128 * (chk + 1)]
                            rhs = qslab[par][64 * half:64 * half + 64,
                                             512 * pair + 128 * chk:512 * pair + 128 * (chk + 1)]
                            w = [ev_q[pair], ev_k[pair]]
                            if len(ev_heads) >= 2:
                                w.append(ev_heads[-2])
                            ev_sc = mm(scb[:, 128 * chk:128 * (chk + 1)], lhs, rhs, True, True,
                                       waits=w, mark=(chk == 3))
                        ev_e1 = act(ph["abuf"][:, 0:512], scb[:, :], AF.Exp,
                                    bias=negc[:, pair:pair + 1], scale=0.125,
                                    waits=[ev_sc, ev_negc] + ([war["ab"][0]] if war["ab"][0] else []))
                        vop(lambda e: e.tensor_tensor(ph["abuf"][:, 0:512],
                                                      ph["abuf"][:, 0:512],
                                                      ph["msk"][:, :], ALU.mult),
                            waits=[ev_e1, ph["ev_msk"]], mark=False)
                        ev_msk = vsync()
                        ev_exp = [ev_msk] * 4
                        n_ch = 1
                    else:
                        for chk in range(4):
                            lhs = kslab[par][64 * half:64 * half + 64,
                                             512 * pair + 128 * chk:512 * pair + 128 * (chk + 1)]
                            rhs = qslab[par][64 * half:64 * half + 64, 512 * pair:512 * (pair + 1)]
                            w = [ev_q[pair], ev_k[pair]]
                            if len(ev_heads) >= 2:
                                w.append(ev_heads[-2])
                            if chk >= 2 and ev_exp[chk - 2] is not None:
                                w.append(ev_exp[chk - 2])
                            ev_mm_sc = mm(ps_sc[chk % 2][:, :], lhs, rhs, True, True,
                                          waits=w, mark=True)
                            ev_exp[chk] = act(ph["abuf"][:, 512 * chk:512 * (chk + 1)],
                                              ps_sc[chk % 2][:, :], AF.Exp,
                                              bias=negc[:, pair:pair + 1], scale=0.125,
                                              waits=[ev_mm_sc, ev_negc] +
                                                    ([war["ab"][chk]] if war["ab"][chk] else []))
                        n_ch = 4
                    ev_av = None
                    for qc in range(4):
                        for chk in range(n_ch):
                            if ph["phase"] == 1:
                                lhs = ph["abuf"][:, 128 * qc:128 * (qc + 1)]
                                vsub = qc
                            else:
                                lhs = ph["abuf"][:, 512 * chk + 128 * qc:512 * chk + 128 * (qc + 1)]
                                vsub = chk
                            rhs = vslab[:, 1040 * vsub + 65 * h:1040 * vsub + 65 * (h + 1)]
                            ev_av = mm(ob[:, 65 * qc:65 * (qc + 1)], lhs, rhs, chk == 0,
                                       chk == n_ch - 1, waits=[ev_exp[chk], ev_vlast],
                                       mark=(chk == n_ch - 1 and qc == 3))
                    ev_rec = [vop(lambda e, qc=qc, ob=ob: e.reciprocal(
                        linv[:, qc:qc + 1], ob[:, 65 * qc + 64:65 * qc + 65]),
                        waits=[ev_av]) for qc in range(4)]
                    ev_stt = None
                    for qc in range(4):
                        vv = vslab[:, 1040 * qc + 65 * h:1040 * qc + 65 * h + 64]
                        dst = ph["oslab"][:, 1024 * qc + 64 * h:1024 * qc + 64 * (h + 1)]
                        w = [ev_rec[qc]] + ([war["oslab"]] if war["oslab"] and h == 0 else [])
                        ev_stt = vop(lambda e, dst=dst, ob=ob, qc=qc, vv=vv:
                                     e.scalar_tensor_tensor(dst, ob[:, 65 * qc:65 * qc + 64],
                                                            linv[:, qc:qc + 1], vv,
                                                            ALU.mult, ALU.subtract), waits=w)
                    ev_heads.append(ev_stt)
                    war["ab"] = [ev_stt] * 4
                war["oslab"] = None
                ev_attn = vsync()
                war["negc"] = ev_attn
                if "dbg_o" in dsc and s == _DEBUG.get("dbg_s", 0) and ph["phase"] == 1:
                    war["dbg"].append(dma("sync", dsc["dbg_v"], vslab[:, :], waits=[ev_attn]))
                    war["dbg"].append(dma("sync", dsc["dbg_o"], ph["oslab"][:, :], waits=[ev_attn]))
                    war["dbg"].append(dma("sync", dsc["dbg_a"], ph["abuf"][:, 0:512], waits=[ev_attn]))

                ev_ot = []
                ev_tr = None
                for k in range(8):
                    for qc in range(4):
                        w = [ev_attn, ev_prm]
                        if war["ot"] and k == 0 and qc == 0:
                            w.append(war["ot"])
                        if k > 0 and qc == 0:
                            w.append(ev_ot[k - 1])
                        ev_tr = mm(ps_tr[:, 128 * qc:128 * (qc + 1)],
                                   ph["oslab"][:, 1024 * qc + 128 * k:1024 * qc + 128 * (k + 1)],
                                   id_sb[:, :], True, True, waits=w, mark=(qc == 3))
                    ev_ot.append(act(ph["ot"][:, 512 * k:512 * (k + 1)], ps_tr[:, :], AF.Copy,
                                     waits=[ev_tr]))
                war["ot"] = None

                ev_proj = []
                ev_pm_last = None
                for c in range(DC):
                    bank = ps_big[c % 2]
                    ev_pm = None
                    for k in range(8):
                        w = [ev_ot[k], ph["ev_w"]]
                        if k == 0 and ev_bigfree[c % 2] is not None:
                            w.append(ev_bigfree[c % 2])
                        ev_pm = mm(bank[:, :],
                                   ph["wp"][:, 1024 * k + 128 * c:1024 * k + 128 * (c + 1)],
                                   ph["ot"][:, 512 * k:512 * (k + 1)], k == 0, k == 7,
                                   waits=w, mark=(k == 7))
                    ev_pm_last = ev_pm
                    ev_ev = act(ph["ymha"][:, 512 * c:512 * (c + 1)], bank[:, :], AF.Identity,
                                bias=prm[:, prm_slots[ph["pb"]] + c:prm_slots[ph["pb"]] + c + 1],
                                waits=[ev_pm, ev_prm] + ([war["ymha"]] if war["ymha"] and c == 0 else []))
                    ev_bigfree[c % 2] = ev_ev
                    ev_proj.append(vop(lambda e, c=c, src2=esl(c): e.tensor_tensor(
                        ph["ymha"][:, 512 * c:512 * (c + 1)], ph["ymha"][:, 512 * c:512 * (c + 1)],
                        src2, ALU.add), waits=[ev_ev]))
                war["ymha"] = None

                def mha_ln_out(c, ev_rb, ev_mrb):
                    ev1 = vop(lambda e, c=c: e.tensor_tensor(
                        tsc[c % 2][:, :], ph["ymha"][:, 512 * c:512 * (c + 1)], rb[:, :],
                        ALU.mult), waits=[ev_rb])
                    ev2 = vop(lambda e, c=c: e.tensor_tensor(
                        tsc[c % 2][:, :], tsc[c % 2][:, :], mrb[:, :], ALU.subtract),
                        waits=[ev1, ev_mrb])
                    w3 = [ev2] + (war["xmbf_dma"] if c == 0 else [])
                    return vop(lambda e, c=c: e.tensor_scalar(
                        ph["xmbf"][:, 512 * c:512 * (c + 1)], tsc[c % 2][:, :],
                        prm[:, prm_slots[ph["g"]] + c:prm_slots[ph["g"]] + c + 1],
                        prm[:, prm_slots[ph["b"]] + c:prm_slots[ph["b"]] + c + 1],
                        ALU.mult, ALU.add), waits=w3)
                if "dbg_y" in dsc and s == _DEBUG.get("dbg_s", 0) and ph["phase"] == 1:
                    war["dbg"].append(dma("sync", dsc["dbg_y"], ph["ymha"][:, :], waits=ev_proj))
                evs2 = layernorm(ev_proj, lambda c: ph["ymha"][:, 512 * c:512 * (c + 1)],
                                 mha_ln_out)
                if "dbg_xm" in dsc and s == _DEBUG.get("dbg_s", 0) and ph["phase"] == 1:
                    war["dbg"].append(dma("sync", dsc["dbg_xm"], ph["xmbf"][:, :], waits=list(evs2)))
                    _d1 = dma("sync", dsc["dbg_mrb"][:, 0:512], rb[:, :], waits=[evs2[0]])
                    _d2 = dma("sync", dsc["dbg_mrb"][:, 512:1024], mrb[:, :], waits=[evs2[0]])
                    _d3 = dma("sync", dsc["dbg_mrows"], rows[0:1, :], waits=[evs2[0]])
                    war_rb.extend([_d1, _d2])
                    war_rows.append(_d3)
                    war["dbg"].extend([_d1, _d2, _d3])
                for c in range(DC):
                    sp_evs[s][c] = dma("sync", ph["spill"][128 * c:128 * (c + 1), 512 * s:512 * (s + 1)],
                                       ph["xmbf"][:, 512 * c:512 * (c + 1)], waits=[evs2[c]])
                war["qk"][par] = ev_attn
                war["v"] = ev_attn
                war["oslab"] = ev_ot[-1]
                war["ot"] = ev_pm_last
                war["ymha"] = evs2[-1]
                war["e"][par] = ev_proj[-1]
                war["xmbf_dma"] = list(sp_evs[s])
            return sp_evs

        def ffn_stage(ph):
            war = dict(xin=None, h=[None, None], u=None, ubf=[], fsc=[None, None])
            ev_last = None
            inplace = ph["ubf"] is ph["u"]
            for s in range(NSL):
                ev_x = []
                for c in range(DC):
                    ev_x.append(dma("sync", ph["xin"][:, 512 * c:512 * (c + 1)],
                                    ph["spill"][128 * c:128 * (c + 1), 512 * s:512 * (s + 1)],
                                    waits=[ph["sp_evs"][s][c]] + ([war["xin"]] if war["xin"] and c == 0 else [])))
                ev_x = [bar("sync", [e]) for e in ev_x]
                dbgf = "dbg_f_x" in dsc and s == 0 and ph.get("dbgf")
                if dbgf:
                    dma("sync", dsc["dbg_f_x"], ph["xin"][:, :], waits=list(ev_x))
                ev_evac = [None] * DC
                ev_wm_last = None
                for halfI in range(2):
                    ev_h = [None] * 16
                    ev_bigfree = [None, None]
                    for f16 in range(16):
                        fc = 16 * halfI + f16
                        bank = ps_big[f16 % 2]
                        ev_m = None
                        for c in range(DC):
                            w = [ph["ev_w1"], ev_x[c]]
                            if c == 0 and ev_bigfree[f16 % 2] is not None:
                                w.append(ev_bigfree[f16 % 2])
                            if c == 0 and f16 == 0 and war["h"][halfI] is not None:
                                w.append(war["h"][halfI])
                            ev_m = mm(bank[:, :],
                                      ph["w1"][:, 4096 * c + 128 * fc:4096 * c + 128 * (fc + 1)],
                                      ph["xin"][:, 512 * c:512 * (c + 1)], c == 0, c == DC - 1,
                                      waits=w, mark=(c == DC - 1))
                        ev_h[f16] = act(ph["h"][:, 512 * f16:512 * (f16 + 1)], bank[:, :], AF.Relu,
                                        bias=prm[:, prm_slots[ph["b1"]] + fc:prm_slots[ph["b1"]] + fc + 1],
                                        waits=[ev_m, ev_prm])
                        ev_bigfree[f16 % 2] = ev_h[f16]
                    if halfI == 0:
                        ev_scfree = [None, None]
                    else:
                        ev_scfree = [ev_evac[DC - 2], ev_evac[DC - 1]]
                    for c in range(DC):
                        bank = ps_sc[c % 2]
                        ev_m = None
                        for f16 in range(16):
                            fc = 16 * halfI + f16
                            w = [ph["ev_w2"], ev_h[f16]]
                            if f16 == 0 and ev_scfree[c % 2] is not None:
                                w.append(ev_scfree[c % 2])
                            ev_m = mm(bank[:, :],
                                      ph["w2"][:, 1024 * fc + 128 * c:1024 * fc + 128 * (c + 1)],
                                      ph["h"][:, 512 * f16:512 * (f16 + 1)], f16 == 0, f16 == 15,
                                      waits=w, mark=(f16 == 15))
                        ev_wm_last = ev_m
                        if halfI == 0:
                            wu = [ev_m, ev_prm]
                            if c == 0 and war["u"]:
                                wu.append(war["u"])
                            if c == 0 and war.get("u_dma"):
                                wu.extend(war["u_dma"])
                            ev_evac[c] = act(ph["u"][:, 512 * c:512 * (c + 1)], bank[:, :], AF.Identity,
                                             bias=prm[:, prm_slots[ph["b2"]] + c:prm_slots[ph["b2"]] + c + 1],
                                             waits=wu)
                            ev_scfree[c % 2] = ev_evac[c]
                        else:
                            evf = act(fsc[c % 2][:, :], bank[:, :], AF.Copy,
                                      waits=[ev_m] + ([war["fsc"][c % 2]] if war["fsc"][c % 2] else []))
                            ev_ad = vop(lambda e, c=c: e.tensor_tensor(
                                ph["u"][:, 512 * c:512 * (c + 1)], ph["u"][:, 512 * c:512 * (c + 1)],
                                fsc[c % 2][:, :], ALU.add), waits=[evf, ev_evac[c]])
                            war["fsc"][c % 2] = ev_ad
                            ev_evac[c] = ev_ad
                            ev_scfree[c % 2] = evf
                    war["h"] = [ev_wm_last, ev_wm_last]
                if dbgf:
                    dma("sync", dsc["dbg_f_h"][:, 0:4096], ph["h"][:, 0:4096],
                        waits=[ev_evac[0], ev_evac[1]])
                ev_y = []
                for c in range(DC):
                    ev_y.append(vop(lambda e, c=c: e.tensor_tensor(
                        ph["u"][:, 512 * c:512 * (c + 1)], ph["u"][:, 512 * c:512 * (c + 1)],
                        ph["xin"][:, 512 * c:512 * (c + 1)], ALU.add), waits=[ev_evac[c]]))
                war["xin"] = ev_y[-1]
                if dbgf:
                    dma("sync", dsc["dbg_f_u"], ph["u"][:, :], waits=list(ev_y))

                def ffn_ln_out(c, ev_rb, ev_mrb):
                    ev1 = vop(lambda e, c=c: e.tensor_tensor(
                        tsc[c % 2][:, :], ph["u"][:, 512 * c:512 * (c + 1)], rb[:, :], ALU.mult),
                        waits=[ev_rb])
                    ev2 = vop(lambda e, c=c: e.tensor_tensor(
                        tsc[c % 2][:, :], tsc[c % 2][:, :], mrb[:, :], ALU.subtract),
                        waits=[ev1, ev_mrb])
                    w = [ev2] + (war["ubf"] if c == 0 else [])
                    ev3 = vop(lambda e, c=c: e.tensor_scalar(
                        ph["ubf"][:, 512 * c:512 * (c + 1)], tsc[c % 2][:, :],
                        prm[:, prm_slots[ph["g"]] + c:prm_slots[ph["g"]] + c + 1],
                        prm[:, prm_slots[ph["b"]] + c:prm_slots[ph["b"]] + c + 1],
                        ALU.mult, ALU.add), waits=w, mark=not ph.get("post_pe"))
                    return vsync() if ph.get("post_pe") else ev3
                evs = layernorm(ev_y, lambda c: ph["u"][:, 512 * c:512 * (c + 1)], ffn_ln_out)
                war["u"] = evs[-1]
                war["ubf"] = list(ph["post"](s, evs))
                if inplace:
                    war["u"] = None
                    war["u_dma"] = war["ubf"]
                ev_last = war["ubf"]
            return ev_last

        p.stage_tag = "s1"
        with ExitStack() as _sctx:
            wq_s = _sctx.enter_context(nc.sbuf_tensor("wq1s", [128, DC * 1024], BF16))
            wk_s = _sctx.enter_context(nc.sbuf_tensor("wk1s", [128, DC * 1024], BF16))
            wv_s = _sctx.enter_context(nc.sbuf_tensor("wv1s", [128, DC * 1024], BF16))
            pw_s = _sctx.enter_context(nc.sbuf_tensor("pw1s", [128, DC * 1024], BF16))
            xbf1 = _sctx.enter_context(nc.sbuf_tensor("xbf1", [128, DC * 512], BF16))
            esl0 = _sctx.enter_context(nc.sbuf_tensor("esl0", [128, 4096], BF16))
            esl1 = _sctx.enter_context(nc.sbuf_tensor("esl1", [128, 4096], BF16))
            qs0 = _sctx.enter_context(nc.sbuf_tensor("qsl0", [128, 8 * 512], BF16))
            qs1 = _sctx.enter_context(nc.sbuf_tensor("qsl1", [128, 8 * 512], BF16))
            ks0 = _sctx.enter_context(nc.sbuf_tensor("ksl0", [128, 8 * 512], BF16))
            ks1 = _sctx.enter_context(nc.sbuf_tensor("ksl1", [128, 8 * 512], BF16))
            vs0 = _sctx.enter_context(nc.sbuf_tensor("vsl", [128, 4 * 1040], BF16))
            oslab1 = _sctx.enter_context(nc.sbuf_tensor("osl1", [128, 4096], BF16))
            ot1 = _sctx.enter_context(nc.sbuf_tensor("ot1", [128, 4096], BF16))
            ymha1 = _sctx.enter_context(nc.sbuf_tensor("ym1", [128, 4096], F32))
            xmbf1 = _sctx.enter_context(nc.sbuf_tensor("xmb1", [128, 4096], BF16))
            abuf1 = _sctx.enter_context(nc.sbuf_tensor("ab1", [128, 512], BF16))
            msk1 = _sctx.enter_context(nc.sbuf_tensor("msk1", [128, 512], BF16))
            qstA = [_sctx.enter_context(nc.sbuf_tensor(f"qstA{i}", [128, 512], F32))
                    for i in range(4)]
            _evm0 = dma("sync", msk1[:, :], mask512)
            _evm0 = dma("sync", msk1[:, 448:512], mask512[:, 448:512], waits=[_evm0])
            ev_msk = bar("sync", [_evm0])
            ev_wl = []
            for slab, wsrc in [(wq_s, din["wq_la"]), (wk_s, din["wk_la"]),
                               (wv_s, din["wv_la"]), (pw_s, din["pw_la"])]:
                for c in range(DC):
                    ev_wl.append(dma("gpsimd", slab[:, 1024 * c:1024 * (c + 1)],
                                     wsrc[128 * c:128 * (c + 1), :]))
            ev_w = bar("gpsimd", ev_wl)
            ev_pc = {}
            for nm in ("w1_lf", "w2_lf", "wq_sa", "wk_sa", "wv_sa", "pw_sa", "w1_pf", "w2_pf"):
                ev_pc[nm] = dma("gpsimd", dsc[nm + "_bf"], din[nm])

            if "dbg_w" in dsc:
                dma("sync", dsc["dbg_w"], wq_s[:, :], waits=[ev_w])
            p.stage_tag = "qA"
            ev_qA = _quant_passA(p, din, qstA, qcol, dma, act, vop, bar, [])
            p.stage_tag = "s1"
            ph1 = dict(phase=1, wq=wq_s, wk=wk_s, wv=wv_s, wp=pw_s, ev_w=ev_w,
                       qs=[qs0, qs1], ks=[ks0, ks1], vs=vs0, esl=[esl0, esl1],
                       oslab=oslab1, ot=ot1, ymha=ymha1, xmbf=xmbf1, xbf=xbf1,
                       abuf=abuf1, msk=msk1, ev_msk=ev_msk,
                       pb="pb_la", g="g_la", b="b_la", spill=dsc["xm1"])
            ev_sp1 = mha_stage(ph1)
            fence_s1 = [p.last_ev[e] for e in ("tensor", "vector", "scalar")
                        if p.last_ev[e]] + [dma_drain("sync")]

        if upto == "s1":
            finish()
            return nc

        p.stage_tag = "s2"
        with ExitStack() as _sctx:
            w1s = _sctx.enter_context(nc.sbuf_tensor("w1s2", [128, DC * 4096], BF16))
            w2s = _sctx.enter_context(nc.sbuf_tensor("w2s2", [128, FC * 1024], BF16))
            xin2 = _sctx.enter_context(nc.sbuf_tensor("xin2", [128, 4096], BF16))
            h2 = _sctx.enter_context(nc.sbuf_tensor("h2", [128, 16 * 512], BF16))
            usl2 = _sctx.enter_context(nc.sbuf_tensor("u2sl", [128, 4096], F32))
            ubf2 = _sctx.enter_context(nc.sbuf_tensor("ubf2", [128, 4096], BF16))
            ccs0 = _sctx.enter_context(nc.sbuf_tensor("ccs0", [128, 512], BF16))
            ccs1 = _sctx.enter_context(nc.sbuf_tensor("ccs1", [128, 512], BF16))
            evl = [dma("sync", w1s[:, 4096 * c:4096 * (c + 1)],
                       dsc["w1_lf_bf"][128 * c:128 * (c + 1), :],
                       waits=[ev_pc["w1_lf"]] + fence_s1) for c in range(DC)]
            ev_w1 = bar("sync", evl)
            evl = [dma("sync", w2s[:, 1024 * fc:1024 * (fc + 1)],
                       dsc["w2_lf_bf"][128 * fc:128 * (fc + 1), :],
                       waits=[ev_pc["w2_lf"]]) for fc in range(FC)]
            ev_w2 = bar("sync", evl)

            war_cc = [None, None, None]
            def post_lf(s, evs):
                last = []
                for tsub in range(4):
                    ev_tra = ev_trb = None
                    for c in range(8):
                        bank = ps_o if c < 4 else ps_tr
                        co = 128 * (c % 4)
                        w = [evs[c], ev_prm]
                        if war_cc[2] and c == 0:
                            w.append(war_cc[2])
                        ev_t = mm(bank[:, co:co + 128],
                                  ubf2[:, 512 * c + 128 * tsub:512 * c + 128 * (tsub + 1)],
                                  id_sb[:, :], True, True, waits=w, mark=(c % 4 == 3))
                        if c == 3:
                            ev_tra = ev_t
                        elif c == 7:
                            ev_trb = ev_t
                    vop(lambda e: e.tensor_copy(ccs0[:, :], ps_o[:, :]),
                        waits=[ev_tra] + ([war_cc[0]] if war_cc[0] else []), mark=False)
                    eva = vsync()
                    vop(lambda e: e.tensor_copy(ccs1[:, :], ps_tr[:, :]),
                        waits=[ev_trb] + ([war_cc[1]] if war_cc[1] else []), mark=False)
                    evb = vsync()
                    r0 = 512 * s + 128 * tsub
                    da = dma("sync", dsc["cc_in"][r0:r0 + 128, 0:512], ccs0[:, :], waits=[eva])
                    db = dma("sync", dsc["cc_in"][r0:r0 + 128, 512:1024], ccs1[:, :], waits=[evb])
                    war_cc[0], war_cc[1], war_cc[2] = da, db, evb
                    last = [da, db]
                return last

            ph2 = dict(w1=w1s, w2=w2s, ev_w1=ev_w1, ev_w2=ev_w2, xin=xin2, h=h2, u=usl2,
                       ubf=ubf2, b1="b1_lf", b2="b2_lf", g="g_lf", b="b_lf",
                       spill=dsc["xm1"], sp_evs=ev_sp1, post=post_lf, post_pe=True)
            ffn_stage(ph2)
            fence_s2 = [p.last_ev[e] for e in ("tensor", "vector", "scalar")
                        if p.last_ev[e]] + [dma_drain("sync")]

        if upto == "s2":
            finish()
            return nc

        p.stage_tag = "ex"
        p.waits("gpsimd", fence_s2)
        ev_cc = p.do("gpsimd", lambda e: e.collective_compute(
            "AllGather", ALU.bypass, replica_groups=[list(range(NCORES))],
            ins=[dsc["cc_in"].tensor.ap().opt()], outs=[dsc["cc_out"].tensor.ap().opt()]),
            cc=True)
        if _DEBUG.get("spill_out"):
            dsc["dbg_ccin"] = nc.dram_tensor("dbg_ccin", [T, D], BF16, kind="ExternalOutput").ap()
            dma("sync", dsc["dbg_ccin"], dsc["cc_in"], waits=[ev_cc])

        with ExitStack() as _sctx:
            gth = _sctx.enter_context(nc.sbuf_tensor("gth", [128, 16384], BF16))
            ev_g = []
            for j in range(16):
                il, qq = j // 4, j % 4
                def g_dma(e, il=il, qq=qq, j=j):
                    v = dsc["cc_out"].rearrange("(m i) x -> i m x", i=32)
                    vg = v[ds(4 * pidc[0], 4)]
                    return e.dma_start(out=gth[:, 1024 * j:1024 * (j + 1)],
                                       in_=vg[il, ds(128 * qq, 128)])
                p.waits("scalar", [ev_cc])
                ev_g.append(p.do("scalar", g_dma, mark=True, dma=True))
            ev_e2tm_l = [dma("scalar", dsc["e2tm"][128 * j:128 * (j + 1), :],
                            gth[:, 1024 * j:1024 * (j + 1)], waits=[ev_g[j]]) for j in range(16)]
            ev_e2tm = bar("scalar", ev_e2tm_l)
            ev_gth = list(ev_g) + ev_e2tm_l

        if upto == "ex":
            finish()
            return nc

        with ExitStack() as _sctx:
            qst0 = _sctx.enter_context(nc.sbuf_tensor("qst0", [128, 512], F32))
            qst1 = _sctx.enter_context(nc.sbuf_tensor("qst1", [128, 512], F32))
            qbf = _sctx.enter_context(nc.sbuf_tensor("qbf", [128, 512], BF16))
            qsc = _sctx.enter_context(nc.sbuf_tensor("qsc", [128, 512], F32))
            qst = [qst0, qst1]

            p.stage_tag = "s3"
            with ExitStack() as _sctx:
                wq3 = _sctx.enter_context(nc.sbuf_tensor("wq3s", [128, DC * 1024], BF16))
                wk3 = _sctx.enter_context(nc.sbuf_tensor("wk3s", [128, DC * 1024], BF16))
                wv3 = _sctx.enter_context(nc.sbuf_tensor("wv3s", [128, DC * 1024], BF16))
                pw3 = _sctx.enter_context(nc.sbuf_tensor("pw3s", [128, DC * 1024], BF16))
                esl0b = _sctx.enter_context(nc.sbuf_tensor("esl0b", [128, 4096], BF16))
                esl1b = _sctx.enter_context(nc.sbuf_tensor("esl1b", [128, 4096], BF16))
                qs0b = _sctx.enter_context(nc.sbuf_tensor("qsl0b", [128, 8 * 512], BF16))
                qs1b = _sctx.enter_context(nc.sbuf_tensor("qsl1b", [128, 8 * 512], BF16))
                ks0b = _sctx.enter_context(nc.sbuf_tensor("ksl0b", [128, 8 * 512], BF16))
                ks1b = _sctx.enter_context(nc.sbuf_tensor("ksl1b", [128, 8 * 512], BF16))
                vs0b = _sctx.enter_context(nc.sbuf_tensor("vslb", [128, 4 * 1040], BF16))
                oslab3 = _sctx.enter_context(nc.sbuf_tensor("osl3", [128, 4096], BF16))
                ot3 = _sctx.enter_context(nc.sbuf_tensor("ot3", [128, 4096], BF16))
                ymha3 = _sctx.enter_context(nc.sbuf_tensor("ym3", [128, 4096], F32))
                xmbf3 = _sctx.enter_context(nc.sbuf_tensor("xmb3", [128, 4096], BF16))
                abuf3 = _sctx.enter_context(nc.sbuf_tensor("ab3", [128, 2048], BF16))
                evl = []
                for slab, nm in [(wq3, "wq_sa"), (wk3, "wk_sa"), (wv3, "wv_sa"), (pw3, "pw_sa")]:
                    for c in range(DC):
                        evl.append(dma("sync", slab[:, 1024 * c:1024 * (c + 1)],
                                       dsc[nm + "_bf"][128 * c:128 * (c + 1), :],
                                       waits=[ev_pc[nm], ev_e2tm] + fence_s2))
                ev_w3 = bar("sync", evl)
                p.stage_tag = "qB"
                ev_gth_b = bar("gpsimd", ev_gth + [ev_e2tm])
                ev_quant, _ = _quant_passB(p, din, dsc, qst, qbf, qsc, qcol, invg,
                                           qpart, qrow, nrm, ones_f, ones_fr, dma, mm,
                                           vop, act, bar, ps_nm, ev_qA, [ev_gth_b])
                p.stage_tag = "s3"

                ph3 = dict(phase=2, wq=wq3, wk=wk3, wv=wv3, wp=pw3, ev_w=ev_w3,
                           qs=[qs0b, qs1b], ks=[ks0b, ks1b], vs=vs0b, esl=[esl0b, esl1b],
                           oslab=oslab3, ot=ot3, ymha=ymha3, xmbf=xmbf3, xbf=None,
                           abuf=abuf3, msk=None, ev_msk=None, fence=[ev_e2tm],
                           pb="pb_sa", g="g_sa", b="b_sa", spill=dsc["xm2"], ev_e2tm=ev_e2tm)
                ev_sp3 = mha_stage(ph3)
                fence_s3 = [p.last_ev[e] for e in ("tensor", "vector", "scalar")
                            if p.last_ev[e]] + [dma_drain("sync")]

            if upto == "s3":
                ev_quant = None

            if upto == "s3":
                finish()
                return nc

            p.stage_tag = "s4"
            with ExitStack() as _sctx:
                w1s4 = _sctx.enter_context(nc.sbuf_tensor("w1s4", [128, DC * 4096], BF16))
                w2s4 = _sctx.enter_context(nc.sbuf_tensor("w2s4", [128, FC * 1024], BF16))
                xin4 = _sctx.enter_context(nc.sbuf_tensor("xin4", [128, 4096], BF16))
                h4 = _sctx.enter_context(nc.sbuf_tensor("h4", [128, 16 * 512], BF16))
                usl4 = _sctx.enter_context(nc.sbuf_tensor("u4sl", [128, 4096], F32))
                evl = [dma("sync", w1s4[:, 4096 * c:4096 * (c + 1)],
                           dsc["w1_pf_bf"][128 * c:128 * (c + 1), :],
                           waits=[ev_pc["w1_pf"]] + fence_s3) for c in range(DC)]
                ev_w14 = bar("sync", evl)
                evl = [dma("sync", w2s4[:, 1024 * fc:1024 * (fc + 1)],
                           dsc["w2_pf_bf"][128 * fc:128 * (fc + 1), :],
                           waits=[ev_pc["w2_pf"]]) for fc in range(FC)]
                ev_w24 = bar("sync", evl)

                u2_evs = [[None] * DC for _ in range(NSL)]
                def post_pf(s, evs):
                    for c in range(DC):
                        u2_evs[s][c] = dma("sync", dsc["u2"][128 * c:128 * (c + 1), 512 * s:512 * (s + 1)],
                                           usl4[:, 512 * c:512 * (c + 1)], waits=[evs[c]])
                    return u2_evs[s]

                ph4 = dict(w1=w1s4, w2=w2s4, ev_w1=ev_w14, ev_w2=ev_w24, xin=xin4, h=h4,
                           u=usl4, ubf=usl4, b1="b1_pf", b2="b2_pf", g="g_pf", b="b_pf",
                           spill=dsc["xm2"], sp_evs=ev_sp3, post=post_pf)
                ffn_stage(ph4)
                fence_s4 = [p.last_ev[e] for e in ("tensor", "vector", "scalar")
                            if p.last_ev[e]] + [dma_drain("sync"), ev_quant]

        if upto == "s4":
            finish()
            return nc

        p.stage_tag = "s5"
        with ExitStack() as _sctx:
            wq1s = _sctx.enter_context(nc.sbuf_tensor("wq1s5", [128, DC * 4096], BF16))
            wq2s = _sctx.enter_context(nc.sbuf_tensor("wq2s5", [128, FC * 1024], BF16))
            x1s = _sctx.enter_context(nc.sbuf_tensor("x1s", [128, 4096], F32))
            z3s = _sctx.enter_context(nc.sbuf_tensor("z3s", [128, 4096], BF16))
            h5 = _sctx.enter_context(nc.sbuf_tensor("h5", [128, 16 * 512], BF16))
            xs5a = _sctx.enter_context(nc.sbuf_tensor("xs5a", [128, 512], F32))
            xs5b = _sctx.enter_context(nc.sbuf_tensor("xs5b", [128, 512], F32))
            xs5 = [xs5a, xs5b]
            evl = [dma("sync", wq1s[:, 4096 * c:4096 * (c + 1)],
                       dsc["wq1"][128 * c:128 * (c + 1), :],
                       waits=[ev_quant] + fence_s4) for c in range(DC)]
            ev_wq1 = bar("sync", evl)
            evl = [dma("sync", wq2s[:, 1024 * fc:1024 * (fc + 1)],
                       dsc["wq2"][128 * fc:128 * (fc + 1), :],
                       waits=[ev_quant]) for fc in range(FC)]
            ev_wq2 = bar("sync", evl)

            ev_out = []
            war5 = dict(x1=[None] * DC, z3=None, h=[None, None], xs=[None, None],
                        fsc=[None, None])
            for s in range(NSL):
                ev_x1 = []
                for c in range(DC):
                    evu = dma("sync", x1s[:, 512 * c:512 * (c + 1)],
                              dsc["u2"][128 * c:128 * (c + 1), 512 * s:512 * (s + 1)],
                              waits=[u2_evs[s][c]] + ([war5["x1"][c]] if war5["x1"][c] else []))
                    evx = dma("sync", xs5[c % 2][:, :],
                              din["x_str_t"][128 * c:128 * (c + 1), 512 * s:512 * (s + 1)],
                              waits=[war5["xs"][c % 2]] if war5["xs"][c % 2] else [])
                    ev_ux = bar("sync", [evu, evx])
                    ev_x1.append(vop(lambda e, c=c: e.tensor_tensor(
                        x1s[:, 512 * c:512 * (c + 1)], x1s[:, 512 * c:512 * (c + 1)],
                        xs5[c % 2][:, :], ALU.add), waits=[ev_ux]))
                    war5["xs"][c % 2] = ev_x1[-1]

                def ln5_out(c, ev_rb, ev_mrb):
                    ev1 = vop(lambda e, c=c: e.tensor_tensor(
                        tsc[c % 2][:, :], x1s[:, 512 * c:512 * (c + 1)], rb[:, :], ALU.mult),
                        waits=[ev_rb, ev_x1[c]])
                    w = [ev1, ev_mrb] + ([war5["z3"]] if war5["z3"] and c == 0 else [])
                    vop(lambda e, c=c: e.tensor_tensor(
                        z3s[:, 512 * c:512 * (c + 1)], tsc[c % 2][:, :], mrb[:, :],
                        ALU.subtract), waits=w, mark=False)
                    return vsync()
                evz = layernorm(ev_x1, lambda c: x1s[:, 512 * c:512 * (c + 1)], ln5_out)
                war5["z3"] = None

                ev_evac = [None] * DC
                ev_last_ad = None
                for halfI in range(2):
                    ev_h = [None] * 16
                    ev_bigfree = [None, None]
                    for f16 in range(16):
                        fc = 16 * halfI + f16
                        bank = ps_big[f16 % 2]
                        ev_m = None
                        for c in range(DC):
                            w = [ev_wq1, evz[c]]
                            if c == 0 and ev_bigfree[f16 % 2] is not None:
                                w.append(ev_bigfree[f16 % 2])
                            if c == 0 and f16 == 0 and war5["h"][halfI] is not None:
                                w.append(war5["h"][halfI])
                            ev_m = mm(bank[:, :],
                                      wq1s[:, 4096 * c + 128 * fc:4096 * c + 128 * (fc + 1)],
                                      z3s[:, 512 * c:512 * (c + 1)], c == 0, c == DC - 1,
                                      waits=w, mark=(c == DC - 1))
                        ev_h[f16] = act(h5[:, 512 * f16:512 * (f16 + 1)], bank[:, :], AF.Relu,
                                        bias=prm[:, prm_slots["b1_q"] + fc:prm_slots["b1_q"] + fc + 1],
                                        waits=[ev_m, ev_prm])
                        ev_bigfree[f16 % 2] = ev_h[f16]
                    if halfI == 0:
                        ev_scfree = [None, None]
                    else:
                        ev_scfree = [ev_evac[DC - 2], ev_evac[DC - 1]]
                    ev_wm_last = None
                    for c in range(DC):
                        bank = ps_sc[c % 2]
                        ev_m = None
                        for f16 in range(16):
                            fc = 16 * halfI + f16
                            w = [ev_wq2, ev_h[f16]]
                            if f16 == 0 and ev_scfree[c % 2] is not None:
                                w.append(ev_scfree[c % 2])
                            ev_m = mm(bank[:, :],
                                      wq2s[:, 1024 * fc + 128 * c:1024 * fc + 128 * (c + 1)],
                                      h5[:, 512 * f16:512 * (f16 + 1)], f16 == 0, f16 == 15,
                                      waits=w, mark=(f16 == 15))
                        ev_wm_last = ev_m
                        evf = act(fsc[c % 2][:, :], bank[:, :],
                                  AF.Identity if halfI == 0 else AF.Copy,
                                  bias=(prm[:, prm_slots["b2_q"] + c:prm_slots["b2_q"] + c + 1]
                                        if halfI == 0 else 0.0),
                                  waits=[ev_m, ev_prm] +
                                        ([war5["fsc"][c % 2]] if war5["fsc"][c % 2] else []))
                        ev_scfree[c % 2] = evf
                        wadd = [evf, evz[c]] + war_st[0] + st_sqb[0]
                        if halfI == 1:
                            wadd.append(ev_evac[c])
                        ev_ad = vop(lambda e, c=c: e.tensor_tensor(
                            x1s[:, 512 * c:512 * (c + 1)], x1s[:, 512 * c:512 * (c + 1)],
                            fsc[c % 2][:, :], ALU.add), waits=wadd)
                        war5["fsc"][c % 2] = ev_ad
                        if halfI == 0:
                            ev_evac[c] = ev_ad
                        else:
                            ev_last_ad = ev_ad
                            evo = dma("sync", out_d[128 * c:128 * (c + 1), 512 * s:512 * (s + 1)],
                                      x1s[:, 512 * c:512 * (c + 1)], waits=[ev_ad])
                            war5["x1"][c] = evo
                            ev_out.append(evo)
                    war5["h"] = [ev_wm_last, ev_wm_last]
                war5["z3"] = ev_h[15]
            for eo in ev_out:
                p.waits("sync", [eo])

        for _e in ("sync", "scalar", "gpsimd"):
            for _i in range(P.NCH):
                _s = f"d{_e}{_i}"
                if p.cnt[_s] > 0:
                    p.wait(_e, (_s, p.cnt[_s]))
        finish()
    return nc


def _quant_passA(p, din, qst, qcol, dma, act, vop, bar, fence):
    nb = len(qst)
    war = [None] * nb
    chunks = [(nm, r, sB) for nm, nrows, nsub in (("w1t_q", 8, 8), ("w2t_q", 32, 2))
              for r in range(nrows) for sB in range(nsub)]
    ci = 0
    for b0 in range(0, len(chunks), nb):
        batch = chunks[b0:b0 + nb]
        evds = []
        for k, (nm, r, sB) in enumerate(batch):
            evds.append(dma("gpsimd", qst[k][:, :],
                            din[nm][128 * r:128 * (r + 1), 512 * sB:512 * (sB + 1)],
                            waits=([war[k]] if war[k] else fence)))
        evb = bar("gpsimd", evds)
        for k, (nm, r, sB) in enumerate(batch):
            eva = act(qst[k][:, :], qst[k][:, :], AF.Abs, waits=[evb])
            evr = vop(lambda e, k=k, ci=ci: e.reduce_sum(
                qcol[:, ci:ci + 1], qst[k][:, :], axis=AX.X), waits=[eva])
            war[k] = evr
            ci += 1
    return war[:2]


def _quant_passB(p, din, dsc, qst, qbf, qsc, qcol, invg, qpart, qrow, nrm, ones_f,
                 ones_fr, dma, mm, vop, act, bar, ps_nm, ev_qA, fence):
    ev_r = []
    for mi, (c0, n) in enumerate([(0, 64), (64, 64)]):
        ev_p = vop(lambda e, mi=mi, c0=c0, n=n: e.reduce_sum(
            qpart[:, mi:mi + 1], qcol[:, c0:c0 + n], axis=AX.X),
            waits=[ev_qA[0], ev_qA[1]])
        d1 = dma("gpsimd", dsc["bc_q"][mi:mi + 1, :], qpart[:, mi:mi + 1], waits=[ev_p])
        d2 = dma("gpsimd", qrow[0:1, :], dsc["bc_q"][mi:mi + 1, :],
                 waits=[d1] + ([ev_r[0]] if mi else []))
        d2 = bar("gpsimd", [d2])
        evs = vop(lambda e, mi=mi: e.reduce_sum(
            nrm[0:1, 40 + mi:41 + mi], qrow[0:1, :], axis=AX.X),
            waits=[d2])
        ev1 = vop(lambda e, mi=mi: e.tensor_scalar(
            nrm[0:1, 44 + mi:45 + mi], nrm[0:1, 40 + mi:41 + mi],
            1.0 / 4194304.0, 1e-5, ALU.mult, ALU.add), waits=[evs])
        ev_r.append(vop(lambda e, mi=mi: e.tensor_scalar(
            nrm[0:1, 46 + mi:47 + mi], nrm[0:1, 44 + mi:45 + mi],
            0.5, None, ALU.mult), waits=[ev1]))
        ev_r.append(vop(lambda e, mi=mi: e.tensor_scalar(
            nrm[0:1, 48 + mi:49 + mi], nrm[0:1, 44 + mi:45 + mi],
            -0.5, None, ALU.mult), waits=[ev1]))
    ev_w = dma("gpsimd", dsc["bc_iv"], nrm[0:1, 46:50], waits=ev_r)
    ev_gb = dma("gpsimd", invg[:, 0:4], dsc["bc_iv"].broadcast_to([128, 4]), waits=[ev_w])
    ev_gb = dma("gpsimd", invg[:, 0:4], dsc["bc_iv"].broadcast_to([128, 4]), waits=[ev_gb])
    p.waits("gpsimd", [ev_gb])
    ev_gb = p.mark_now("gpsimd")
    ev_g = [ev_gb, ev_gb]
    war = [None, None]
    war_qsc = [None]
    last = None
    ci = 0
    chunks = []
    for pi, (nm, nrows, nsub) in enumerate((("w1t_q", 8, 8), ("w2t_q", 32, 2))):
        dstd = dsc["wq1"] if nm == "w1t_q" else dsc["wq2"]
        for r in range(nrows):
            for sB in range(nsub):
                chunks.append((pi, dstd, r, sB))
    for b0 in range(0, len(chunks), 2):
        batch = chunks[b0:b0 + 2]
        evds = []
        for k, (pi, dstd, r, sB) in enumerate(batch):
            nm = "w1t_q" if pi == 0 else "w2t_q"
            evds.append(dma("gpsimd", qst[k][:, :],
                            din[nm][128 * r:128 * (r + 1), 512 * sB:512 * (sB + 1)],
                            waits=[war[k]] if war[k] else fence))
        evb = bar("gpsimd", evds)
        for k, (pi, dstd, r, sB) in enumerate(batch):
            stg = qst[k][:, :]
            v1 = vop(lambda e, stg=stg, pi=pi: e.tensor_scalar(
                qsc[:, :], stg, invg[:, pi:pi + 1], None, ALU.is_gt),
                waits=[evb, ev_g[pi]] + ([war_qsc[0]] if war_qsc[0] else []))
            v2 = vop(lambda e, stg=stg, pi=pi: e.tensor_scalar(
                stg, stg, invg[:, 2 + pi:3 + pi], None, ALU.is_lt), waits=[v1])
            v3 = vop(lambda e, stg=stg: e.tensor_tensor(
                qbf[:, :], qsc[:, :], stg, ALU.subtract),
                waits=[v2] + ([last] if last else []))
            war_qsc[0] = v3
            last = dma("gpsimd", dstd[128 * r:128 * (r + 1), 512 * sB:512 * (sB + 1)],
                       qbf[:, :], waits=[v3])
            war[k] = last
    return last, ev_r[-1]


_ncache = {}
_LAST_RES = None

def kernel(**inputs):
    global _LAST_RES
    in_maps = _layout_inputs(inputs)
    if "nc" not in _ncache:
        _ncache["nc"] = build_nc()
    res = run_bass_kernel_spmd(_ncache["nc"], in_maps, core_ids=list(range(NCORES)))
    _LAST_RES = res.results
    return _assemble(res.results)
